# version 05
# baseline (speedup 1.0000x reference)
import contextlib
import numpy as np
import ml_dtypes
import concourse.bass as bass
import concourse.mybir as mybir
from concourse.bass_utils import run_bass_kernel_spmd

F32 = mybir.dt.float32
BF16 = mybir.dt.bfloat16
I32 = mybir.dt.int32
ALU = mybir.AluOpType
ACT = mybir.ActivationFunctionType

D = 2048
SEQ = 8192
DIN = 4096
NST = 16
KCV = 4
HID = 5632
NC = 8
TC = SEQ // NC
TB = 512
NB = TC // TB
EPS = 1e-5
BF = ml_dtypes.bfloat16

ENGS = ("sync", "vector", "scalar", "tensor", "gpsimd")


class Prog:
    def __init__(self, nc, stack, extra=()):
        self.nc = nc
        self.stack = stack
        self.ev = {k: [] for k in ENGS}
        self.sem = {}
        self.ctr = {}
        for name in ("te", "ve", "ac") + tuple(extra):
            self.sem[name] = stack.enter_context(nc.semaphore("s_" + name))
            self.ctr[name] = 0

    def emit(self, eng, fn, inc=None, dma=False):
        amt = 16 if dma else 1
        if inc is not None:
            self.ctr[inc] += amt
        sem = self.sem[inc] if inc is not None else None

        def run(e, fn=fn, sem=sem, amt=amt):
            inst = fn(e)
            if sem is not None:
                inst.then_inc(sem, amt)
        self.ev[eng].append(run)
        return self.ctr[inc] if inc is not None else None

    AUDIT = False

    def wait(self, eng, name, val):
        if val <= 0:
            return
        if Prog.AUDIT and name not in ("te", "ve", "ac") and val != self.ctr[name]:
            print(f"NONCUM wait: eng={eng} sem={name} val={val} cur={self.ctr[name]}")
        sem = self.sem[name]
        self.ev[eng].append(lambda e, s=sem, v=val: e.wait_ge(s, v))

    def replay(self):
        with self.nc.Block() as blk:
            blk.sync(lambda e: [f(e) for f in self.ev["sync"]])
            blk.vector(lambda e: [f(e) for f in self.ev["vector"]])
            blk.scalar(lambda e: [f(e) for f in self.ev["scalar"]])
            blk.tensor(lambda e: [f(e) for f in self.ev["tensor"]])
            blk.gpsimd(lambda e: [f(e) for f in self.ev["gpsimd"]])



def build_l1():
    nc = bass.Bass("TRN2", target_bir_lowering=False, debug=False,
                   detect_race_conditions=False)
    xT = nc.dram_tensor("xT", [128, 16, TC + 3], F32, kind="ExternalInput").ap()
    w_inx = nc.dram_tensor("w_inx", [16, 128, 16, 256], F32, kind="ExternalInput").ap()
    convw = nc.dram_tensor("convw", [128, 32, KCV], F32, kind="ExternalInput").ap()
    w_xp = nc.dram_tensor("w_xp", [128, 32, 160], F32, kind="ExternalInput").ap()
    w_dt = nc.dram_tensor("w_dt", [128, DIN], F32, kind="ExternalInput").ap()
    dt_b = nc.dram_tensor("dt_b", [128, 32], F32, kind="ExternalInput").ap()

    delta_o = nc.dram_tensor("delta_o", [128, 32, TC], F32, kind="ExternalOutput").ap()
    u_o = nc.dram_tensor("u_o", [128, 32, TC], BF16, kind="ExternalOutput").ap()
    xc_o = nc.dram_tensor("xc_o", [128, 32, TC], BF16, kind="ExternalOutput").ap()
    xn_o = nc.dram_tensor("xn_o", [128, 16, TC + 3], BF16, kind="ExternalOutput").ap()
    b_o = nc.dram_tensor("b_o", [16, TC], F32, kind="ExternalOutput").ap()
    c_o = nc.dram_tensor("c_o", [16, TC], F32, kind="ExternalOutput").ap()
    srow = nc.dram_tensor("srow", [2, TB], F32, kind="Internal").ap()

    stack = contextlib.ExitStack()
    with stack:
        sb = lambda n, s, d: stack.enter_context(nc.sbuf_tensor(n, s, d))
        psf = lambda n, s: stack.enter_context(nc.psum_tensor(n, s, F32))
        wq4 = sb("wq4", [128, 16, 1024], BF16)
        fscr = sb("fscr", [128, 5120], F32)
        xpart = sb("xpart", [128, 16, TB + 16], F32)
        bfa = sb("bfa", [128, 16, TB + 16], BF16)
        bfb = sb("bfb", [128, 16, TB + 16], BF16)
        xnb = sb("xnb", [128, 16, TB + 16], BF16)
        xcb = sb("xcb", [128, 8, TB], BF16)
        xcv2 = [sb("xcv0", [128, TB], F32), sb("xcv1", [128, TB], F32)]
        xdbl = sb("xdbl", [128, 2, TC], F32)
        dtr = sb("dtr", [128, TC], BF16)
        wxp = sb("wxp", [128, 32, 160], BF16)
        wdt = sb("wdt", [128, DIN], BF16)
        cw = sb("cw", [128, 32, KCV], F32)
        dtb = sb("dtb", [128, 32], F32)
        stat = sb("stat", [1, 8, TB], F32)
        mrep = sb("mrep", [128, TB], F32)
        rrep = sb("rrep", [128, TB], F32)
        dsc0 = sb("dsc0", [128, TB], F32)
        dsc1 = sb("dsc1", [128, TB], F32)
        dblk = sb("dblk", [128, TB], F32)
        ublk = sb("ublk", [128, TB], BF16)
        onesb = sb("onesb", [128, 1], BF16)
        pA = [psf("pA0", [128, TB]), psf("pA1", [128, TB])]
        pHh = [psf("pH0", [128, 16]), psf("pH1", [128, 16])]
        pst0 = psf("pst0", [1, TB])
        pst1 = psf("pst1", [1, TB])
        px = psf("px", [128, TB])

        p = Prog(nc, stack, extra=("ws", "sw", "xt", "gst", "gxn", "xa", "gxc", "xc0", "xc1", "gout"))

        p.emit("vector", lambda e: e.memset(onesb[:], 1.0), inc="ve")
        p.emit("sync", lambda e: e.dma_start(
            out=fscr[:, 0:5120].rearrange("p (m e) -> p m e", m=32), in_=w_xp), inc="ws", dma=True)
        mw0 = p.ctr["ws"]
        p.wait("vector", "ws", mw0)
        p.emit("vector", lambda e: e.tensor_copy(
            out=wxp[:], in_=fscr[:, 0:5120].rearrange("p (m e) -> p m e", m=32)), inc="ve")
        mwxp = p.ctr["ve"]
        p.wait("sync", "ve", mwxp)
        p.emit("sync", lambda e: e.dma_start(out=fscr[:, 0:4096], in_=w_dt), inc="ws", dma=True)
        mw1 = p.ctr["ws"]
        p.wait("vector", "ws", mw1)
        p.emit("vector", lambda e: e.tensor_copy(out=wdt[:], in_=fscr[:, 0:4096]), inc="ve")
        mwdt = p.ctr["ve"]
        p.emit("sync", lambda e: e.dma_start(out=cw[:], in_=convw), inc="sw", dma=True)
        p.emit("sync", lambda e: e.dma_start(out=dtb[:], in_=dt_b), inc="sw", dma=True)
        msw = p.ctr["sw"]
        p.wait("vector", "sw", msw)
        p.wait("scalar", "sw", msw)

        groups = [(0, TB), (TB, TB), (2 * TB, 3)]
        m_norm_done = 0
        m_stats_used = 0
        m_xn_hbm = 0
        for (c0, w) in groups:
            p.wait("sync", "ve", m_norm_done)
            p.emit("sync", lambda e, c0=c0, w=w: e.dma_start(
                out=xpart[:, 0:8, 0:w], in_=xT[:, 0:8, c0:c0 + w]), inc="xt", dma=True)
            p.emit("sync", lambda e, c0=c0, w=w: e.dma_start(
                out=xpart[:, 8:16, 0:w], in_=xT[:, 8:16, c0:c0 + w]), inc="xt", dma=True)
            mldx = p.ctr["xt"]
            p.wait("vector", "xt", mldx)
            p.emit("vector", lambda e, w=w: e.tensor_copy(out=bfa[:, :, 0:w], in_=xpart[:, :, 0:w]), inc="ve")
            mxbf = p.ctr["ve"]
            p.wait("scalar", "ve", mxbf)
            p.emit("scalar", lambda e, w=w: e.activation(out=bfb[:, :, 0:w], in_=bfa[:, :, 0:w], func=ACT.Square), inc="ac")
            msq = p.ctr["ac"]
            p.wait("tensor", "ve", max(mxbf, 1))
            p.wait("tensor", "ac", msq)
            p.wait("tensor", "ve", m_stats_used)
            for pdst, src in ((pst0, bfa), (pst1, bfb)):
                for k in range(16):
                    p.emit("tensor", lambda e, pdst=pdst, src=src, k=k, w=w: e.matmul(
                        out=pdst[0:1, 0:w], lhsT=onesb[:], rhs=src[:, k, 0:w],
                        start=(k == 0), stop=(k == 15)),
                        inc="te" if (k == 15) else None)
            mst = p.ctr["te"]
            p.wait("vector", "te", mst)
            p.emit("vector", lambda e, w=w: e.tensor_scalar_mul(out=stat[0:1, 0, 0:w], in0=pst0[0:1, 0:w], scalar1=1.0 / D))
            p.emit("vector", lambda e, w=w: e.tensor_scalar_mul(out=stat[0:1, 1, 0:w], in0=pst1[0:1, 0:w], scalar1=1.0 / D))
            p.emit("vector", lambda e, w=w: e.tensor_tensor(out=stat[0:1, 2, 0:w], in0=stat[0:1, 0, 0:w], in1=stat[0:1, 0, 0:w], op=ALU.mult))
            p.emit("vector", lambda e, w=w: e.tensor_tensor(out=stat[0:1, 3, 0:w], in0=stat[0:1, 1, 0:w], in1=stat[0:1, 2, 0:w], op=ALU.subtract))
            p.emit("vector", lambda e, w=w: e.tensor_scalar_add(out=stat[0:1, 3, 0:w], in0=stat[0:1, 3, 0:w], scalar1=EPS), inc="ve")
            mvar = p.ctr["ve"]
            m_stats_used = mvar
            p.wait("scalar", "ve", mvar)
            p.emit("scalar", lambda e, w=w: e.activation(out=stat[0:1, 4, 0:w], in_=stat[0:1, 3, 0:w], func=ACT.Sqrt), inc="ac")
            msqr = p.ctr["ac"]
            p.wait("vector", "ac", msqr)
            p.emit("vector", lambda e, w=w: e.reciprocal(out=stat[0:1, 5, 0:w], in_=stat[0:1, 4, 0:w]), inc="ve")
            mrs = p.ctr["ve"]
            p.wait("gpsimd", "ve", mrs)
            p.emit("gpsimd", lambda e, w=w: e.dma_start(out=srow[0:1, 0:w], in_=stat[0:1, 0, 0:w]), inc="gst", dma=True)
            p.emit("gpsimd", lambda e, w=w: e.dma_start(out=srow[1:2, 0:w], in_=stat[0:1, 5, 0:w]), inc="gst", dma=True)
            mb1 = p.ctr["gst"]
            p.wait("gpsimd", "gst", mb1)
            p.emit("gpsimd", lambda e, w=w: e.dma_start(out=mrep[:, 0:w], in_=srow[0:1, 0:w].partition_broadcast(128)), inc="gst", dma=True)
            p.emit("gpsimd", lambda e, w=w: e.dma_start(out=rrep[:, 0:w], in_=srow[1:2, 0:w].partition_broadcast(128)), inc="gst", dma=True)
            mb2 = p.ctr["gst"]
            p.wait("vector", "gst", mb2)
            for k in range(16):
                p.emit("vector", lambda e, k=k, w=w: e.tensor_tensor(
                    out=xpart[:, k, 0:w], in0=xpart[:, k, 0:w], in1=mrep[:, 0:w], op=ALU.subtract))
                p.emit("vector", lambda e, k=k, w=w: e.tensor_tensor(
                    out=xnb[:, k, 0:w], in0=xpart[:, k, 0:w], in1=rrep[:, 0:w], op=ALU.mult))
            p.emit("vector", lambda e: e.engine_nop(), inc="ve")
            m_norm_done = p.ctr["ve"]
            p.wait("gpsimd", "ve", m_norm_done)
            p.emit("gpsimd", lambda e, c0=c0, w=w: e.dma_start(
                out=xn_o[:, 0:8, c0:c0 + w], in_=xnb[:, 0:8, 0:w]), inc="gxn", dma=True)
            p.emit("gpsimd", lambda e, c0=c0, w=w: e.dma_start(
                out=xn_o[:, 8:16, c0:c0 + w], in_=xnb[:, 8:16, 0:w]), inc="gxn", dma=True)
            m_xn_hbm = p.ctr["gxn"]

        m_wcast = mwdt
        m_xn_used = 0
        m_xc_hbm = 0
        m_xdbl_ve = 0
        m_conv_done = m_norm_done
        evac_marks = [0] * 8
        m_xproj_te = 0
        for q in range(4):
            for g in range(4):
                gj = 4 * q + g
                p.wait("sync", "ve", m_wcast)
                p.emit("sync", lambda e, gj=gj: e.dma_start(
                    out=fscr[:, 0:4096].rearrange("p (k c) -> p k c", k=16), in_=w_inx[gj]), inc="ws", dma=True)
                mw = p.ctr["ws"]
                p.wait("vector", "ws", mw)
                if g == 0:
                    p.wait("vector", "te", m_xn_used)
                p.emit("vector", lambda e, g=g: e.tensor_copy(
                    out=wq4[:, :, 256 * g:256 * (g + 1)], in_=fscr[:, 0:4096].rearrange("p (k c) -> p k c", k=16)), inc="ve")
                m_wcast = p.ctr["ve"]
            for b in range(NB):
                p.wait("sync", "gxn", m_xn_hbm)
                p.wait("sync", "te", m_xn_used)
                p.emit("sync", lambda e, b=b: e.dma_start(
                    out=xnb[:, 0:8, 0:TB + 3], in_=xn_o[:, 0:8, TB * b:TB * b + TB + 3]), inc="xa", dma=True)
                p.emit("sync", lambda e, b=b: e.dma_start(
                    out=xnb[:, 8:16, 0:TB + 3], in_=xn_o[:, 8:16, TB * b:TB * b + TB + 3]), inc="xa", dma=True)
                mxn = p.ctr["xa"]
                p.wait("tensor", "ve", m_wcast)
                p.wait("tensor", "xa", mxn)
                p.wait("tensor", "ve", m_conv_done)
                for m in range(8):
                    sl = m % 2
                    if m >= 2:
                        p.wait("tensor", "ac", evac_marks[m - 2])
                    for k in range(16):
                        p.emit("tensor", lambda e, m=m, k=k, sl=sl: e.matmul(
                            out=pA[sl][:, 0:TB], lhsT=wq4[:, k, 128 * m:128 * (m + 1)],
                            rhs=xnb[:, k, 3:TB + 3], start=(k == 0), stop=(k == 15)))
                    for k in range(16):
                        p.emit("tensor", lambda e, m=m, k=k, sl=sl: e.matmul(
                            out=pHh[sl][:, 0:3], lhsT=wq4[:, k, 128 * m:128 * (m + 1)],
                            rhs=xnb[:, k, 0:3], start=(k == 0), stop=(k == 15)),
                            inc="te" if k == 15 else None)
                    mte = p.ctr["te"]
                    p.wait("scalar", "te", mte)
                    p.emit("scalar", lambda e, m=m, sl=sl: e.activation(
                        out=xpart[:, m, 0:3], in_=pHh[sl][:, 0:3], func=ACT.Identity))
                    p.emit("scalar", lambda e, m=m, sl=sl: e.activation(
                        out=xpart[:, m, 3:TB + 3], in_=pA[sl][:, 0:TB], func=ACT.Identity), inc="ac")
                    evac_marks[m] = p.ctr["ac"]
                m_xn_used = p.ctr["te"]
                m_evac = p.ctr["ac"]
                p.wait("vector", "ac", m_evac)
                p.wait("scalar", "te", m_xproj_te)
                p.wait("scalar", "gxc", m_xc_hbm)
                sil_marks = [0, 0]
                for m in range(8):
                    mc = 8 * q + m
                    xv = xcv2[m % 2]
                    p.wait("vector", "ac", sil_marks[m % 2])
                    p.emit("vector", lambda e, mc=mc, m=m, xv=xv: e.tensor_scalar_mul(
                        out=xv[:, :], in0=xpart[:, m, 0:TB], scalar1=cw[:, mc, 0:1]))
                    for k in range(1, KCV):
                        p.emit("vector", lambda e, m=m, mc=mc, k=k, xv=xv: e.scalar_tensor_tensor(
                            out=xv[:, :], in0=xpart[:, m, k:TB + k], scalar=cw[:, mc, k:k + 1],
                            in1=xv[:, :], op0=ALU.mult, op1=ALU.add), inc="ve" if k == KCV - 1 else None)
                    mcv = p.ctr["ve"]
                    p.wait("scalar", "ve", mcv)
                    p.emit("scalar", lambda e, m=m, xv=xv: e.activation(
                        out=xcb[:, m, :], in_=xv[:, :], func=ACT.Silu), inc="ac")
                    sil_marks[m % 2] = p.ctr["ac"]
                p.emit("vector", lambda e: e.engine_nop(), inc="ve")
                m_conv_done = p.ctr["ve"]
                msil = p.ctr["ac"]
                p.wait("gpsimd", "ac", msil)
                p.emit("gpsimd", lambda e, q=q, b=b: e.dma_start(
                    out=xc_o[:, 8 * q:8 * (q + 1), TB * b:TB * (b + 1)], in_=xcb[:, :, :]),
                    inc="gxc", dma=True)
                m_xc_hbm = p.ctr["gxc"]
                p.wait("tensor", "ac", msil)
                for mo, mwd in ((0, 128), (128, 32)):
                    p.wait("tensor", "ve", m_xdbl_ve)
                    for k in range(8):
                        p.emit("tensor", lambda e, mo=mo, mwd=mwd, k=k, q=q: e.matmul(
                            out=px[0:mwd, 0:TB], lhsT=wxp[:, 8 * q + k, mo:mo + mwd],
                            rhs=xcb[:, k, :], start=(k == 0), stop=(k == 7)),
                            inc="te" if k == 7 else None)
                    mxp = p.ctr["te"]
                    p.wait("vector", "te", mxp)
                    ch = 0 if mo == 0 else 1
                    if q == 0:
                        p.emit("vector", lambda e, ch=ch, mwd=mwd, b=b: e.tensor_copy(
                            out=xdbl[0:mwd, ch, TB * b:TB * (b + 1)], in_=px[0:mwd, 0:TB]), inc="ve")
                    else:
                        p.emit("vector", lambda e, ch=ch, mwd=mwd, b=b: e.tensor_tensor(
                            out=xdbl[0:mwd, ch, TB * b:TB * (b + 1)], in0=xdbl[0:mwd, ch, TB * b:TB * (b + 1)],
                            in1=px[0:mwd, 0:TB], op=ALU.add), inc="ve")
                    m_xdbl_ve = p.ctr["ve"]
                m_xproj_te = p.ctr["te"]

        p.wait("vector", "ve", m_xdbl_ve)
        p.emit("vector", lambda e: e.tensor_copy(out=dtr[:], in_=xdbl[:, 0, :]), inc="ve")
        mdtr = p.ctr["ve"]
        p.wait("gpsimd", "ve", m_xdbl_ve)
        p.emit("gpsimd", lambda e: e.dma_start(out=b_o, in_=xdbl[0:16, 1, :]), inc="gout", dma=True)
        p.emit("gpsimd", lambda e: e.dma_start(out=c_o, in_=xdbl[16:32, 1, :]), inc="gout", dma=True)

        m_px_free = 0
        m_u_prev = 0
        m_store_prev = 0
        for m in range(32):
            for b in range(NB):
                par = (m * NB + b) % 2
                p.wait("sync", "gxc", m_xc_hbm)
                p.wait("sync", "ve", m_u_prev)
                xcsem = "xc0" if par == 0 else "xc1"
                p.emit("sync", lambda e, m=m, b=b, par=par: e.dma_start(
                    out=bfa[:, par, 0:TB], in_=xc_o[:, m, TB * b:TB * (b + 1)]), inc=xcsem, dma=True)
                mxc = p.ctr[xcsem]
                p.wait("tensor", "ve", mdtr)
                p.wait("tensor", "ac", m_px_free)
                p.emit("tensor", lambda e, m=m, b=b: e.matmul(
                    out=px[:, 0:TB], lhsT=wdt[:, 128 * m:128 * (m + 1)],
                    rhs=dtr[:, TB * b:TB * (b + 1)], start=True, stop=True), inc="te")
                mdt = p.ctr["te"]
                p.wait("scalar", "te", mdt)
                p.wait("scalar", "ve", m_u_prev)
                p.emit("scalar", lambda e, m=m: e.activation(
                    out=dsc0[:, :], in_=px[:, 0:TB], func=ACT.Abs, bias=dtb[:, m:m + 1]))
                p.emit("scalar", lambda e, m=m: e.activation(
                    out=dsc1[:, :], in_=px[:, 0:TB], func=ACT.Relu, bias=dtb[:, m:m + 1]), inc="ac")
                m_px_free = p.ctr["ac"]
                p.emit("scalar", lambda e: e.activation(
                    out=dsc0[:, :], in_=dsc0[:, :], func=ACT.Exp, scale=-1.0))
                p.emit("scalar", lambda e: e.activation(
                    out=dsc0[:, :], in_=dsc0[:, :], func=ACT.Ln, bias=1.0), inc="ac")
                mln = p.ctr["ac"]
                p.wait("vector", "ac", mln)
                p.wait("vector", "gout", m_store_prev)
                p.emit("vector", lambda e: e.tensor_tensor(
                    out=dblk[:, :], in0=dsc0[:, :], in1=dsc1[:, :], op=ALU.add), inc="ve")
                p.wait("vector", xcsem, mxc)
                p.emit("vector", lambda e, par=par: e.tensor_tensor(
                    out=ublk[:, :], in0=dblk[:, :], in1=bfa[:, par, 0:TB], op=ALU.mult), inc="ve")
                m_u_prev = p.ctr["ve"]
                p.wait("gpsimd", "ve", m_u_prev)
                p.emit("gpsimd", lambda e, m=m, b=b: e.dma_start(
                    out=delta_o[:, m, TB * b:TB * (b + 1)], in_=dblk[:, :]), inc="gout", dma=True)
                p.emit("gpsimd", lambda e, m=m, b=b: e.dma_start(
                    out=u_o[:, m, TB * b:TB * (b + 1)], in_=ublk[:, :]), inc="gout", dma=True)
                m_store_prev = p.ctr["gout"]

        for nmm in ("gout", "gxn", "gxc", "gst"):
            p.wait("gpsimd", nmm, p.ctr[nmm])
            p.wait("sync", nmm, p.ctr[nmm])
        p.replay()
    return nc



def build_l2():
    W = 4096
    nc = bass.Bass("TRN2", target_bir_lowering=False, debug=False,
                   detect_race_conditions=False)
    delta_i = nc.dram_tensor("delta_i", [128, 4, SEQ], F32, kind="ExternalInput").ap()
    u_i = nc.dram_tensor("u_i", [128, 4, SEQ], BF16, kind="ExternalInput").ap()
    alog_i = nc.dram_tensor("alog_i", [128, 4, NST], F32, kind="ExternalInput").ap()
    b_i = nc.dram_tensor("b_i", [16, SEQ], F32, kind="ExternalInput").ap()
    c_i = nc.dram_tensor("c_i", [16, SEQ], F32, kind="ExternalInput").ap()
    y_o = nc.dram_tensor("y_o", [128, 4, SEQ], F32, kind="ExternalOutput").ap()
    bimg = nc.dram_tensor("bimg", [16, SEQ], BF16, kind="Internal").ap()
    cimg = nc.dram_tensor("cimg", [16, SEQ], BF16, kind="Internal").ap()

    stack = contextlib.ExitStack()
    with stack:
        sb = lambda n, s, d: stack.enter_context(nc.sbuf_tensor(n, s, d))
        dl = sb("dl", [128, W], F32)
        uu = sb("uu", [128, W], BF16)
        dA = sb("dA", [128, W], F32)
        uB = sb("uB", [128, W], BF16)
        h0f = sb("h0f", [128, W], F32)
        h1b = sb("h1b", [128, W], BF16)
        hc0 = sb("hc0", [128, W], BF16)
        hc1 = sb("hc1", [128, W], BF16)
        s2b = sb("s2b", [128, W], BF16)
        s4b = sb("s4b", [128, W], BF16)
        s8b = sb("s8b", [128, W], BF16)
        yb = sb("yb", [128, W], F32)
        brep = [sb("brep0", [128, W], BF16), sb("brep1", [128, W], BF16)]
        crep = [sb("crep0", [128, W], BF16), sb("crep1", [128, W], BF16)]
        bcs = sb("bcs", [16, SEQ // 2], F32)
        bcb = sb("bcb", [16, SEQ // 2], BF16)
        alog = sb("alog", [128, 4, NST], F32)
        aval = sb("aval", [128, 4, NST], F32)
        state = sb("state", [128, 4, NST], F32)

        p = Prog(nc, stack, extra=("sw", "gi", "d1", "u1", "r0", "r1", "gy"))
        p.emit("sync", lambda e: e.dma_start(out=alog[:], in_=alog_i), inc="sw", dma=True)
        ma = p.ctr["sw"]
        p.wait("scalar", "sw", ma)
        p.emit("scalar", lambda e: e.activation(out=aval[:], in_=alog[:], func=ACT.Exp), inc="ac")
        mae = p.ctr["ac"]
        p.wait("vector", "ac", mae)
        p.emit("vector", lambda e: e.tensor_scalar_mul(out=aval[:], in0=aval[:], scalar1=-1.0), inc="ve")
        mave = p.ctr["ve"]
        HS = SEQ // 2
        for src, dst in ((b_i, bimg), (c_i, cimg)):
            for hh in range(2):
                p.wait("sync", "gi", p.ctr["gi"])
                p.emit("sync", lambda e, src=src, hh=hh: e.dma_start(
                    out=bcs[:], in_=src[:, HS * hh:HS * (hh + 1)]), inc="sw", dma=True)
                p.wait("vector", "sw", p.ctr["sw"])
                p.emit("vector", lambda e: e.tensor_copy(out=bcb[:], in_=bcs[:]), inc="ve")
                p.wait("gpsimd", "ve", p.ctr["ve"])
                p.emit("gpsimd", lambda e, dst=dst, hh=hh: e.dma_start(
                    out=dst[:, HS * hh:HS * (hh + 1)], in_=bcb[:]), inc="gi", dma=True)
        m_img = p.ctr["gi"]

        m_dl_used = 0
        m_uu_used = 0
        m_dA_used = 0
        m_rep_used = [0, 0]
        m_y_done = 0
        it = 0
        for c in range(4):
            for w in range(2):
                p.wait("sync", "ac", m_dl_used)
                p.emit("sync", lambda e, c=c, w=w: e.dma_start(
                    out=dl[:], in_=delta_i[:, c, W * w:W * (w + 1)]), inc="d1", dma=True)
                mdl_ = p.ctr["d1"]
                p.wait("sync", "ve", m_uu_used)
                p.emit("sync", lambda e, c=c, w=w: e.dma_start(
                    out=uu[:], in_=u_i[:, c, W * w:W * (w + 1)]), inc="u1", dma=True)
                muu_ = p.ctr["u1"]
                for n in range(NST):
                    sl = it % 2
                    rsem = "r0" if sl == 0 else "r1"
                    p.wait("gpsimd", "gi", m_img)
                    p.wait("gpsimd", "ve", m_rep_used[sl])
                    p.emit("gpsimd", lambda e, n=n, w=w, sl=sl: e.dma_start(
                        out=brep[sl][:], in_=bimg[n:n + 1, W * w:W * (w + 1)].partition_broadcast(128)), inc=rsem, dma=True)
                    p.emit("gpsimd", lambda e, n=n, w=w, sl=sl: e.dma_start(
                        out=crep[sl][:], in_=cimg[n:n + 1, W * w:W * (w + 1)].partition_broadcast(128)), inc=rsem, dma=True)
                    mrep_ = p.ctr[rsem]
                    p.wait("scalar", "d1", mdl_)
                    p.wait("scalar", "ve", max(m_dA_used, mave))
                    p.emit("scalar", lambda e, c=c, n=n: e.activation(
                        out=dA[:], in_=dl[:], func=ACT.Exp, scale=aval[:, c, n:n + 1]), inc="ac")
                    mexp = p.ctr["ac"]
                    if n == NST - 1:
                        m_dl_used = mexp
                    p.wait("vector", "u1", muu_)
                    p.wait("vector", rsem, mrep_)
                    p.emit("vector", lambda e, sl=sl: e.tensor_tensor(
                        out=uB[:], in0=uu[:], in1=brep[sl][:], op=ALU.mult), inc="ve")
                    if n == NST - 1:
                        m_uu_used = p.ctr["ve"]
                    p.wait("vector", "ac", mexp)
                    if w == 0:
                        p.emit("vector", lambda e: e.tensor_tensor_scan(
                            out=h0f[:], data0=dA[:], data1=uB[:], initial=0.0,
                            op0=ALU.mult, op1=ALU.add), inc="ve")
                        m_dA_used = p.ctr["ve"]
                        p.emit("vector", lambda e, c=c, n=n: e.tensor_copy(
                            out=state[:, c, n:n + 1], in_=h0f[:, W - 1:W]))
                        hbuf = h0f
                    else:
                        p.emit("vector", lambda e, c=c, n=n: e.tensor_tensor_scan(
                            out=h1b[:], data0=dA[:], data1=uB[:], initial=state[:, c, n:n + 1],
                            op0=ALU.mult, op1=ALU.add), inc="ve")
                        m_dA_used = p.ctr["ve"]
                        hbuf = h1b
                    hcp = (hc0, hc1)[n % 2]
                    p.emit("vector", lambda e, hbuf=hbuf, hcp=hcp, sl=sl: e.tensor_tensor(
                        out=hcp[:], in0=hbuf[:], in1=crep[sl][:], op=ALU.mult), inc="ve")
                    m_rep_used[sl] = p.ctr["ve"]
                    if n % 2 == 1:
                        if (n // 2) % 2 == 0:
                            p.emit("vector", lambda e: e.tensor_tensor(
                                out=s2b[:], in0=hc0[:], in1=hc1[:], op=ALU.add))
                        else:
                            p.emit("vector", lambda e: e.tensor_tensor(
                                out=hc0[:], in0=hc0[:], in1=hc1[:], op=ALU.add))
                            if (n // 4) % 2 == 0:
                                p.emit("vector", lambda e: e.tensor_tensor(
                                    out=s4b[:], in0=s2b[:], in1=hc0[:], op=ALU.add))
                            else:
                                p.emit("vector", lambda e: e.tensor_tensor(
                                    out=hc0[:], in0=s2b[:], in1=hc0[:], op=ALU.add))
                                if n == 7:
                                    p.emit("vector", lambda e: e.tensor_tensor(
                                        out=s8b[:], in0=s4b[:], in1=hc0[:], op=ALU.add))
                                else:
                                    p.wait("vector", "gy", m_y_done)
                                    p.emit("vector", lambda e: e.tensor_tensor(
                                        out=hc0[:], in0=s4b[:], in1=hc0[:], op=ALU.add))
                                    p.emit("vector", lambda e: e.tensor_tensor(
                                        out=yb[:], in0=s8b[:], in1=hc0[:], op=ALU.add), inc="ve")
                    it += 1
                my = p.ctr["ve"]
                p.wait("gpsimd", "ve", my)
                p.emit("gpsimd", lambda e, c=c, w=w: e.dma_start(
                    out=y_o[:, c, W * w:W * (w + 1)], in_=yb[:]), inc="gy", dma=True)
                m_y_done = p.ctr["gy"]
        for nmm in ("gy", "gi"):
            p.wait("gpsimd", nmm, p.ctr[nmm])
            p.wait("sync", nmm, p.ctr[nmm])
        p.replay()
    return nc



def build_l3():
    TH = 512
    nc = bass.Bass("TRN2", target_bir_lowering=False, debug=False,
                   detect_race_conditions=False)
    xT = nc.dram_tensor("xT", [2, 128, 16, TH], F32, kind="ExternalInput").ap()
    xn_i = nc.dram_tensor("xn_i", [2, 128, 16, TH], BF16, kind="ExternalInput").ap()
    xc_i = nc.dram_tensor("xc_i", [128, 32, TC], BF16, kind="ExternalInput").ap()
    y_i = nc.dram_tensor("y_i", [128, 32, TC], F32, kind="ExternalInput").ap()
    w_inz = nc.dram_tensor("w_inz", [32, 128, 16, 128], F32, kind="ExternalInput").ap()
    w_out = nc.dram_tensor("w_out", [16, 128, 32, 128], F32, kind="ExternalInput").ap()
    wq_g = nc.dram_tensor("wq_g", [44, 128, 16, 128], I32, kind="ExternalInput").ap()
    wq_u = nc.dram_tensor("wq_u", [44, 128, 16, 128], I32, kind="ExternalInput").ap()
    wq_d = nc.dram_tensor("wq_d", [16, 128, 44, 128], I32, kind="ExternalInput").ap()
    s_g = nc.dram_tensor("s_g", [128, 44], F32, kind="ExternalInput").ap()
    s_u = nc.dram_tensor("s_u", [128, 44], F32, kind="ExternalInput").ap()
    s_d = nc.dram_tensor("s_d", [128, 16], F32, kind="ExternalInput").ap()
    dpar = nc.dram_tensor("dpar", [128, 32], F32, kind="ExternalInput").ap()
    o_T = nc.dram_tensor("o_T", [128, 16, TC], F32, kind="ExternalOutput").ap()
    x1sp = nc.dram_tensor("x1sp", [2, 128, 16, TH], F32, kind="Internal").ap()
    srow = nc.dram_tensor("srow", [2, TH], F32, kind="Internal").ap()

    stack = contextlib.ExitStack()
    with stack:
        sb = lambda n, s, d: stack.enter_context(nc.sbuf_tensor(n, s, d))
        psf = lambda n, s: stack.enter_context(nc.psum_tensor(n, s, F32))
        area = sb("area", [128, 44, TH], BF16)
        xn = sb("xn", [128, 16, TH], BF16)
        xn2 = xn
        x1 = sb("x1", [128, 16, TH], F32)
        fstg = [sb("fstg0", [128, 32, 128], F32), sb("fstg1", [128, 32, 128], F32)]
        istg = [sb("istg0", [128, 16, 128], I32), sb("istg1", [128, 16, 128], I32)]
        wt = [sb("wt0", [128, 44, 128], BF16), sb("wt1", [128, 44, 128], BF16)]
        ytl = sb("ytl", [128, TH], F32)
        gtmp = sb("gtmp", [128, TH], F32)
        gt = sb("gt", [128, TH], BF16)
        ut = sb("ut", [128, TH], BF16)
        xct = sb("xct", [128, TH], BF16)
        stat = sb("stat", [1, 4, TH], F32)
        mrep = sb("mrep", [128, TH], F32)
        rrep = sb("rrep", [128, TH], F32)
        cs_rep = sb("cs_rep", [128, TH], F32)
        cs2_rep = sb("cs2_rep", [128, TH], F32)
        sgv = sb("sgv", [128, 44], F32)
        suv = sb("suv", [128, 44], F32)
        sdv = sb("sdv", [128, 16], F32)
        sd128 = sb("sd128", [128, 16], F32)
        dp = sb("dp", [128, 32], F32)
        onesb = sb("onesb", [128, 1], BF16)
        onesf = sb("onesf", [128, 1], F32)
        x1r = sb("x1r", [128, TH], F32)
        otile = sb("otile", [128, TH], F32)
        pA = [psf("pA0", [128, TH]), psf("pA1", [128, TH])]
        pst0 = psf("pst0", [1, TH])
        pst1 = psf("pst1", [1, TH])

        p = Prog(nc, stack, extra=("sw", "xa", "f0", "f1", "i0", "i1", "xr", "yx", "gst", "gx1", "go"))
        p.emit("vector", lambda e: e.memset(onesb[:], 1.0), inc="ve")
        p.emit("vector", lambda e: e.memset(onesf[:], 1.0), inc="ve")
        p.emit("sync", lambda e: e.dma_start(out=sgv[:], in_=s_g), inc="sw", dma=True)
        p.emit("sync", lambda e: e.dma_start(out=suv[:], in_=s_u), inc="sw", dma=True)
        p.emit("sync", lambda e: e.dma_start(out=sdv[:], in_=s_d), inc="sw", dma=True)
        p.emit("sync", lambda e: e.dma_start(out=dp[:], in_=dpar), inc="sw", dma=True)
        msetup = p.ctr["sw"]
        p.wait("vector", "sw", msetup)
        p.emit("vector", lambda e: e.tensor_scalar_mul(out=sd128[:], in0=sdv[:], scalar1=-128.0), inc="ve")
        msd = p.ctr["ve"]
        p.wait("scalar", "sw", msetup)

        m_cast = [0, 0]
        m_wused = [0, 0]
        m_evA = [0, 0]
        m_evV = [0, 0]
        m_xn_used = 0
        m_out_done = 0
        m_stats_used = 0
        m_x1r_used = 0
        m_ytl_used = 0
        m_x1sp = 0

        for th in range(2):
            t0 = TH * th
            p.wait("sync", "te", m_xn_used)
            p.emit("sync", lambda e, th=th: e.dma_start(
                out=xn[:, :, :], in_=xn_i[th]), inc="xa", dma=True)
            mxn = p.ctr["xa"]
            for m in range(32):
                sl = m % 2
                p.wait("sync", "ve", m_cast[sl])
                fsem = "f0" if sl == 0 else "f1"
                p.emit("sync", lambda e, m=m, sl=sl: e.dma_start(
                    out=fstg[sl][:, 0:16, :], in_=w_inz[m]), inc=fsem, dma=True)
                mwl = p.ctr[fsem]
                p.wait("vector", fsem, mwl)
                p.wait("vector", "te", m_wused[sl])
                p.emit("vector", lambda e, sl=sl: e.tensor_copy(out=wt[sl][:, 0:16, :], in_=fstg[sl][:, 0:16, :]), inc="ve")
                m_cast[sl] = p.ctr["ve"]
                p.wait("tensor", "ve", m_cast[sl])
                p.wait("tensor", "xa", mxn)
                p.wait("tensor", "ac", m_evA[sl])
                p.wait("tensor", "ve", m_evV[sl])
                for k in range(16):
                    p.emit("tensor", lambda e, k=k, sl=sl: e.matmul(
                        out=pA[sl][:, :], lhsT=wt[sl][:, k, :], rhs=xn[:, k, :],
                        start=(k == 0), stop=(k == 15)), inc="te" if k == 15 else None)
                m_wused[sl] = p.ctr["te"]
                p.wait("scalar", "te", m_wused[sl])
                p.emit("scalar", lambda e, m=m, sl=sl: e.activation(
                    out=area[:, m, :], in_=pA[sl][:, :], func=ACT.Silu), inc="ac")
                m_evA[sl] = p.ctr["ac"]
            msz = p.ctr["ac"]
            m_xn_used = p.ctr["te"]
            for m in range(32):
                p.wait("sync", "ve", m_ytl_used)
                p.emit("sync", lambda e, m=m, t0=t0: e.dma_start(
                    out=ytl[:, :], in_=y_i[:, m, t0:t0 + TH]), inc="yx", dma=True)
                p.emit("sync", lambda e, m=m, t0=t0: e.dma_start(
                    out=xct[:, :], in_=xc_i[:, m, t0:t0 + TH]), inc="yx", dma=True)
                mxcl = p.ctr["yx"]
                p.wait("vector", "yx", mxcl)
                p.wait("vector", "ac", msz)
                p.emit("vector", lambda e, m=m: e.scalar_tensor_tensor(
                    out=gtmp[:, :], in0=xct[:, :], scalar=dp[:, m:m + 1], in1=ytl[:, :],
                    op0=ALU.mult, op1=ALU.add), inc="ve")
                m_ytl_used = p.ctr["ve"]
                p.emit("vector", lambda e, m=m: e.tensor_tensor(
                    out=area[:, m, :], in0=gtmp[:, :], in1=area[:, m, :], op=ALU.mult), inc="ve")
            my2 = p.ctr["ve"]
            for m in range(16):
                sl = m % 2
                fsem = "f0" if sl == 0 else "f1"
                p.wait("sync", "ve", m_cast[sl])
                p.emit("sync", lambda e, m=m, sl=sl: e.dma_start(
                    out=fstg[sl][:], in_=w_out[m]), inc=fsem, dma=True)
                mwl = p.ctr[fsem]
                p.wait("sync", "ve", m_x1r_used)
                p.emit("sync", lambda e, m=m, th=th: e.dma_start(
                    out=x1r[:, :], in_=xT[th, :, m, :]), inc="xr", dma=True)
                mxl = p.ctr["xr"]
                p.wait("vector", fsem, mwl)
                p.wait("vector", "te", m_wused[sl])
                p.emit("vector", lambda e, sl=sl: e.tensor_copy(out=wt[sl][:, 0:32, :], in_=fstg[sl][:]), inc="ve")
                m_cast[sl] = p.ctr["ve"]
                p.wait("tensor", "ve", max(m_cast[sl], my2))
                p.wait("tensor", "ac", m_evA[sl])
                p.wait("tensor", "ve", m_evV[sl])
                for k in range(32):
                    p.emit("tensor", lambda e, k=k, sl=sl: e.matmul(
                        out=pA[sl][:, :], lhsT=wt[sl][:, k, :], rhs=area[:, k, :],
                        start=(k == 0), stop=(k == 31)), inc="te" if k == 31 else None)
                m_wused[sl] = p.ctr["te"]
                p.wait("vector", "te", m_wused[sl])
                p.wait("vector", "xr", mxl)
                p.emit("vector", lambda e, m=m, sl=sl: e.tensor_tensor(
                    out=x1[:, m, :], in0=pA[sl][:, :], in1=x1r[:, :], op=ALU.add), inc="ve")
                m_evV[sl] = p.ctr["ve"]
                m_x1r_used = p.ctr["ve"]
            mx1 = p.ctr["ve"]
            p.wait("gpsimd", "ve", mx1)
            p.emit("gpsimd", lambda e, th=th: e.dma_start(
                out=x1sp[th], in_=x1[:, :, :]), inc="gx1", dma=True)
            m_x1sp = p.ctr["gx1"]
            p.wait("tensor", "ve", max(mx1, m_stats_used))
            for k in range(16):
                p.emit("tensor", lambda e, k=k: e.matmul(
                    out=pst0[0:1, 0:TH], lhsT=onesf[:], rhs=x1[:, k, :],
                    start=(k == 0), stop=(k == 15)), inc="te" if k == 15 else None)
            for sp2 in range(2):
                c0 = 256 * sp2
                sl = sp2 % 2
                p.wait("scalar", "ve", max(mx1, m_cast[sl]))
                p.emit("scalar", lambda e, c0=c0, sl=sl: e.activation(
                    out=fstg[sl][:].rearrange("p a b -> p (a b)").rearrange("p (a c) -> p a c", a=16),
                    in_=x1[:, :, c0:c0 + 256], func=ACT.Square), inc="ac")
                msq2 = p.ctr["ac"]
                p.wait("tensor", "ac", msq2)
                for k in range(16):
                    p.emit("tensor", lambda e, k=k, c0=c0, sl=sl: e.matmul(
                        out=pst1[0:1, c0:c0 + 256],
                        lhsT=onesf[:],
                        rhs=fstg[sl][:].rearrange("p a b -> p (a b)").rearrange("p (a c) -> p a c", a=16)[:, k, :],
                        start=(k == 0), stop=(k == 15)), inc="te" if k == 15 else None)
                m_cast[sl] = m_cast[sl]
                m_wused[sl] = max(m_wused[sl], p.ctr["te"])
            msx2 = p.ctr["te"]
            p.wait("vector", "te", msx2)
            p.emit("vector", lambda e: e.tensor_scalar_mul(out=stat[0:1, 0, :], in0=pst0[0:1, 0:TH], scalar1=1.0 / D))
            p.emit("vector", lambda e: e.tensor_scalar_mul(out=stat[0:1, 1, :], in0=pst1[0:1, 0:TH], scalar1=1.0 / D))
            p.emit("vector", lambda e: e.tensor_tensor(out=stat[0:1, 2, :], in0=stat[0:1, 0, :], in1=stat[0:1, 0, :], op=ALU.mult))
            p.emit("vector", lambda e: e.tensor_tensor(out=stat[0:1, 2, :], in0=stat[0:1, 1, :], in1=stat[0:1, 2, :], op=ALU.subtract))
            p.emit("vector", lambda e: e.tensor_scalar_add(out=stat[0:1, 2, :], in0=stat[0:1, 2, :], scalar1=EPS), inc="ve")
            mvar = p.ctr["ve"]
            m_stats_used = mvar
            p.wait("scalar", "ve", mvar)
            p.emit("scalar", lambda e: e.activation(out=stat[0:1, 2, :], in_=stat[0:1, 2, :], func=ACT.Sqrt), inc="ac")
            msqr = p.ctr["ac"]
            p.wait("vector", "ac", msqr)
            p.emit("vector", lambda e: e.reciprocal(out=stat[0:1, 2, :], in_=stat[0:1, 2, :]), inc="ve")
            mrs = p.ctr["ve"]
            p.wait("gpsimd", "ve", mrs)
            p.emit("gpsimd", lambda e: e.dma_start(out=srow[0:1, :], in_=stat[0:1, 0, :]), inc="gst", dma=True)
            p.emit("gpsimd", lambda e: e.dma_start(out=srow[1:2, :], in_=stat[0:1, 2, :]), inc="gst", dma=True)
            mb1 = p.ctr["gst"]
            p.wait("gpsimd", "gst", mb1)
            p.emit("gpsimd", lambda e: e.dma_start(out=mrep[:, :], in_=srow[0:1, :].partition_broadcast(128)), inc="gst", dma=True)
            p.emit("gpsimd", lambda e: e.dma_start(out=rrep[:, :], in_=srow[1:2, :].partition_broadcast(128)), inc="gst", dma=True)
            mb2 = p.ctr["gst"]
            p.wait("vector", "gst", mb2)
            p.wait("vector", "te", max(msx2, m_xn_used))
            for k in range(16):
                p.emit("vector", lambda e, k=k: e.tensor_tensor(
                    out=x1[:, k, :], in0=x1[:, k, :], in1=mrep[:, :], op=ALU.subtract))
                p.emit("vector", lambda e, k=k: e.tensor_tensor(
                    out=xn2[:, k, :], in0=x1[:, k, :], in1=rrep[:, :], op=ALU.mult))
            p.emit("vector", lambda e: e.engine_nop(), inc="ve")
            mxn2 = p.ctr["ve"]
            p.wait("tensor", "ve", max(mxn2, m_stats_used))
            for k in range(16):
                p.emit("tensor", lambda e, k=k: e.matmul(
                    out=pst0[0:1, 0:TH], lhsT=onesb[:], rhs=xn2[:, k, :],
                    start=(k == 0), stop=(k == 15)), inc="te" if k == 15 else None)
            mcs = p.ctr["te"]
            p.wait("vector", "te", mcs)
            p.emit("vector", lambda e: e.tensor_scalar_mul(out=stat[0:1, 3, :], in0=pst0[0:1, 0:TH], scalar1=128.0), inc="ve")
            mcsr = p.ctr["ve"]
            m_stats_used = mcsr
            p.wait("gpsimd", "ve", mcsr)
            p.emit("gpsimd", lambda e: e.dma_start(out=srow[0:1, :], in_=stat[0:1, 3, :]), inc="gst", dma=True)
            mb3 = p.ctr["gst"]
            p.wait("gpsimd", "gst", mb3)
            p.emit("gpsimd", lambda e: e.dma_start(out=cs_rep[:, :], in_=srow[0:1, :].partition_broadcast(128)), inc="gst", dma=True)
            mb4 = p.ctr["gst"]
            m_gt_used = 0
            for m in range(44):
                for gi in range(2):
                    wq = (wq_g, wq_u)[gi]
                    sv = (sgv, suv)[gi]
                    fn = (ACT.Silu, ACT.Identity)[gi]
                    sl = (2 * m + gi) % 2
                    isem = "i0" if sl == 0 else "i1"
                    p.wait("sync", "ve", m_cast[sl])
                    p.emit("sync", lambda e, m=m, sl=sl, wq=wq: e.dma_start(
                        out=istg[sl][:, 0:16, :], in_=wq[m]), inc=isem, dma=True)
                    mwl = p.ctr[isem]
                    p.wait("vector", isem, mwl)
                    p.wait("vector", "te", m_wused[sl])
                    p.emit("vector", lambda e, sl=sl: e.tensor_copy(out=wt[sl][:, 0:16, :], in_=istg[sl][:, 0:16, :]), inc="ve")
                    m_cast[sl] = p.ctr["ve"]
                    p.wait("tensor", "ve", max(m_cast[sl], mxn2))
                    p.wait("tensor", "ac", m_evA[sl])
                    p.wait("tensor", "ve", m_evV[sl])
                    for k in range(16):
                        p.emit("tensor", lambda e, k=k, sl=sl: e.matmul(
                            out=pA[sl][:, :], lhsT=wt[sl][:, k, :], rhs=xn2[:, k, :],
                            start=(k == 0), stop=(k == 15)), inc="te" if k == 15 else None)
                    m_wused[sl] = p.ctr["te"]
                    p.wait("vector", "te", m_wused[sl])
                    p.wait("vector", "gst", mb4)
                    p.wait("vector", "ac", m_gt_used)
                    p.emit("vector", lambda e, sl=sl: e.tensor_tensor(
                        out=gtmp[:, :], in0=pA[sl][:, :], in1=cs_rep[:, :], op=ALU.subtract), inc="ve")
                    m_evV[sl] = p.ctr["ve"]
                    mev = p.ctr["ve"]
                    p.wait("scalar", "ve", mev)
                    if gi == 0:
                        p.emit("scalar", lambda e, m=m, sv=sv, fn=fn: e.activation(
                            out=gt[:, :], in_=gtmp[:, :], func=fn, scale=sv[:, m:m + 1]), inc="ac")
                    else:
                        p.emit("scalar", lambda e, m=m, sv=sv, fn=fn: e.activation(
                            out=ut[:, :], in_=gtmp[:, :], func=fn, scale=sv[:, m:m + 1]), inc="ac")
                mgu = p.ctr["ac"]
                p.wait("vector", "ac", mgu)
                p.emit("vector", lambda e, m=m: e.tensor_tensor(
                    out=area[:, m, :], in0=gt[:, :], in1=ut[:, :], op=ALU.mult), inc="ve")
                m_gt_used = mgu
            mguall = p.ctr["ve"]
            p.wait("tensor", "ve", max(mguall, m_stats_used))
            for k in range(44):
                p.emit("tensor", lambda e, k=k: e.matmul(
                    out=pst0[0:1, 0:TH], lhsT=onesb[:], rhs=area[:, k, :],
                    start=(k == 0), stop=(k == 43)), inc="te" if k == 43 else None)
            mcs2 = p.ctr["te"]
            p.wait("vector", "te", mcs2)
            p.emit("vector", lambda e: e.tensor_copy(out=stat[0:1, 3, :], in_=pst0[0:1, 0:TH]), inc="ve")
            mcs2v = p.ctr["ve"]
            m_stats_used = mcs2v
            p.wait("gpsimd", "ve", mcs2v)
            p.emit("gpsimd", lambda e: e.dma_start(out=srow[1:2, :], in_=stat[0:1, 3, :]), inc="gst", dma=True)
            mb5 = p.ctr["gst"]
            p.wait("gpsimd", "gst", mb5)
            p.emit("gpsimd", lambda e: e.dma_start(out=cs2_rep[:, :], in_=srow[1:2, :].partition_broadcast(128)), inc="gst", dma=True)
            mb6 = p.ctr["gst"]
            for m in range(16):
                sl = m % 2
                isem = "i0" if sl == 0 else "i1"
                for gof, gn in ((0, 16), (16, 16), (32, 12)):
                    p.wait("sync", "ve", m_cast[sl])
                    p.emit("sync", lambda e, m=m, sl=sl, gof=gof, gn=gn: e.dma_start(
                        out=istg[sl][:, 0:gn, :], in_=wq_d[m, :, gof:gof + gn, :]), inc=isem, dma=True)
                    mwl = p.ctr[isem]
                    p.wait("vector", isem, mwl)
                    if gof == 0:
                        p.wait("vector", "te", m_wused[sl])
                    p.emit("vector", lambda e, sl=sl, gof=gof, gn=gn: e.tensor_copy(
                        out=wt[sl][:, gof:gof + gn, :], in_=istg[sl][:, 0:gn, :]), inc="ve")
                    m_cast[sl] = p.ctr["ve"]
                p.wait("sync", "ve", m_x1r_used)
                p.wait("sync", "gx1", m_x1sp)
                p.emit("sync", lambda e, m=m, th=th: e.dma_start(
                    out=x1r[:, :], in_=x1sp[th, :, m, :]), inc="xr", dma=True)
                mx1l = p.ctr["xr"]
                p.wait("tensor", "ve", m_cast[sl])
                p.wait("tensor", "ac", m_evA[sl])
                p.wait("tensor", "ve", m_evV[sl])
                for k in range(44):
                    p.emit("tensor", lambda e, k=k, sl=sl: e.matmul(
                        out=pA[sl][:, :], lhsT=wt[sl][:, k, :], rhs=area[:, k, :],
                        start=(k == 0), stop=(k == 43)), inc="te" if k == 43 else None)
                m_wused[sl] = p.ctr["te"]
                p.wait("vector", "te", m_wused[sl])
                p.wait("vector", "gst", mb6)
                p.wait("vector", "xr", mx1l)
                p.wait("vector", "ve", msd)
                p.wait("vector", "go", m_out_done)
                p.emit("vector", lambda e, m=m: e.scalar_tensor_tensor(
                    out=gtmp[:, :], in0=cs2_rep[:, :], scalar=sd128[:, m:m + 1], in1=x1r[:, :],
                    op0=ALU.mult, op1=ALU.add))
                p.emit("vector", lambda e, m=m, sl=sl: e.scalar_tensor_tensor(
                    out=otile[:, :], in0=pA[sl][:, :], scalar=sdv[:, m:m + 1], in1=gtmp[:, :],
                    op0=ALU.mult, op1=ALU.add), inc="ve")
                m_evV[sl] = p.ctr["ve"]
                m_x1r_used = p.ctr["ve"]
                mot = p.ctr["ve"]
                p.wait("gpsimd", "ve", mot)
                p.emit("gpsimd", lambda e, m=m, t0=t0: e.dma_start(
                    out=o_T[:, m, t0:t0 + TH], in_=otile[:, :]), inc="go", dma=True)
                m_out_done = p.ctr["go"]
        for nmm in ("go", "gx1", "gst"):
            p.wait("gpsimd", nmm, p.ctr[nmm])
            p.wait("sync", nmm, p.ctr[nmm])
        p.replay()
    return nc



def _chunk(a, p=128):
    n, t = a.shape
    c = n // p
    return np.ascontiguousarray(a.reshape(c, p, t).transpose(1, 0, 2))


def _unchunk(a):
    p, c, t = a.shape
    return np.ascontiguousarray(a.transpose(1, 0, 2).reshape(c * p, t))


def _vec_chunk(v, p=128):
    return np.ascontiguousarray(v.reshape(-1, p).T)


def _bf(a):
    a = np.asarray(a)
    if a.dtype == np.uint16:
        return a.view(BF)
    return a.astype(BF) if a.dtype != BF else a


def kernel(L0_attn_hidden_states, L0_attn_in_proj_w, L0_attn_conv1d_w,
           L0_attn_x_proj_w, L0_attn_dt_proj_w, L0_attn_dt_proj_bias,
           L0_attn_A_log, L0_attn_D_param, L0_attn_out_proj_w,
           L0_ffn_wq_gate, L0_ffn_s_gate, L0_ffn_wq_up, L0_ffn_s_up,
           L0_ffn_wq_down, L0_ffn_s_down):
    X = np.asarray(L0_attn_hidden_states)[0]
    XT = np.ascontiguousarray(X.T)
    XTp = np.concatenate([np.zeros((D, 3), np.float32), XT], axis=1)

    w_inx0 = _chunk(np.ascontiguousarray(np.asarray(L0_attn_in_proj_w)[:DIN].T))
    w_inx = np.ascontiguousarray(w_inx0.reshape(128, 16, 16, 256).transpose(2, 0, 1, 3))
    convw = _chunk(np.asarray(L0_attn_conv1d_w))
    w_xp = _chunk(np.ascontiguousarray(np.asarray(L0_attn_x_proj_w).T))
    w_dt = np.ascontiguousarray(np.asarray(L0_attn_dt_proj_w).T)
    dt_b = _vec_chunk(np.asarray(L0_attn_dt_proj_bias))

    nc1 = build_l1()
    in1 = []
    for c in range(NC):
        sl = XTp[:, TC * c:TC * c + TC + 3]
        in1.append({
            "xT": _chunk(np.ascontiguousarray(sl)),
            "w_inx": w_inx, "convw": convw, "w_xp": w_xp,
            "w_dt": w_dt, "dt_b": dt_b,
        })
    r1 = run_bass_kernel_spmd(nc1, in1, core_ids=list(range(NC)))
    delta_all = np.concatenate([r1.results[c]["delta_o"] for c in range(NC)], axis=2)
    u_all = np.concatenate([_bf(r1.results[c]["u_o"]) for c in range(NC)], axis=2)
    b_all = np.concatenate([r1.results[c]["b_o"] for c in range(NC)], axis=1)
    c_all = np.concatenate([r1.results[c]["c_o"] for c in range(NC)], axis=1)

    alog = _chunk(np.asarray(L0_attn_A_log).reshape(DIN, NST))
    nc2 = build_l2()
    in2 = []
    for c in range(NC):
        in2.append({
            "delta_i": np.ascontiguousarray(delta_all[:, 4 * c:4 * (c + 1), :]),
            "u_i": np.ascontiguousarray(u_all[:, 4 * c:4 * (c + 1), :]),
            "alog_i": np.ascontiguousarray(alog[:, 4 * c:4 * (c + 1), :]),
            "b_i": b_all, "c_i": c_all,
        })
    r2 = run_bass_kernel_spmd(nc2, in2, core_ids=list(range(NC)))
    y_all = np.concatenate([r2.results[c]["y_o"] for c in range(NC)], axis=1)

    w_inz0 = _chunk(np.ascontiguousarray(np.asarray(L0_attn_in_proj_w)[DIN:].T))
    w_inz = np.ascontiguousarray(w_inz0.reshape(128, 16, 32, 128).transpose(2, 0, 1, 3))
    w_out0 = _chunk(np.ascontiguousarray(np.asarray(L0_attn_out_proj_w).T))
    w_out = np.ascontiguousarray(w_out0.reshape(128, 32, 16, 128).transpose(2, 0, 1, 3))
    wq_g0 = _chunk(np.ascontiguousarray(np.asarray(L0_ffn_wq_gate).T))
    wq_g = np.ascontiguousarray(wq_g0.reshape(128, 16, 44, 128).transpose(2, 0, 1, 3))
    wq_u0 = _chunk(np.ascontiguousarray(np.asarray(L0_ffn_wq_up).T))
    wq_u = np.ascontiguousarray(wq_u0.reshape(128, 16, 44, 128).transpose(2, 0, 1, 3))
    wq_d0 = _chunk(np.ascontiguousarray(np.asarray(L0_ffn_wq_down).T))
    wq_d = np.ascontiguousarray(wq_d0.reshape(128, 44, 16, 128).transpose(2, 0, 1, 3))
    s_g = _vec_chunk(np.asarray(L0_ffn_s_gate))
    s_u = _vec_chunk(np.asarray(L0_ffn_s_up))
    s_d = _vec_chunk(np.asarray(L0_ffn_s_down))
    dpar = _vec_chunk(np.asarray(L0_attn_D_param))

    nc3 = build_l3()
    in3 = []
    for c in range(NC):
        in3.append({
            "xT": np.ascontiguousarray(_chunk(np.ascontiguousarray(XT[:, TC * c:TC * (c + 1)])).reshape(128, 16, 2, 512).transpose(2, 0, 1, 3)),
            "xn_i": np.ascontiguousarray(_bf(r1.results[c]["xn_o"])[:, :, 3:].reshape(128, 16, 2, 512).transpose(2, 0, 1, 3)),
            "xc_i": _bf(r1.results[c]["xc_o"]),
            "y_i": np.ascontiguousarray(y_all[:, :, TC * c:TC * (c + 1)]),
            "w_inz": w_inz, "w_out": w_out,
            "wq_g": wq_g, "wq_u": wq_u, "wq_d": wq_d,
            "s_g": s_g, "s_u": s_u, "s_d": s_d, "dpar": dpar,
        })
    r3 = run_bass_kernel_spmd(nc3, in3, core_ids=list(range(NC)))
    oT = np.concatenate([_unchunk(r3.results[c]["o_T"]) for c in range(NC)], axis=1)
    return np.ascontiguousarray(oT.T).reshape(1, SEQ, D).astype(np.float32)


# version 06
# speedup vs baseline: 1.2072x; 1.2072x over previous
import contextlib
import numpy as np
import ml_dtypes
import concourse.bass as bass
import concourse.mybir as mybir
from concourse.bass_utils import run_bass_kernel_spmd

F32 = mybir.dt.float32
BF16 = mybir.dt.bfloat16
I32 = mybir.dt.int32
ALU = mybir.AluOpType
ACT = mybir.ActivationFunctionType

D = 2048
SEQ = 8192
DIN = 4096
NST = 16
KCV = 4
HID = 5632
NC = 8
TC = SEQ // NC
TB = 512
NB = TC // TB
EPS = 1e-5
BF = ml_dtypes.bfloat16

ENGS = ("sync", "vector", "scalar", "tensor", "gpsimd")


class Prog:
    def __init__(self, nc, stack, extra=()):
        self.nc = nc
        self.stack = stack
        self.ev = {k: [] for k in ENGS}
        self.sem = {}
        self.ctr = {}
        for name in ("te", "ve", "ac") + tuple(extra):
            self.sem[name] = stack.enter_context(nc.semaphore("s_" + name))
            self.ctr[name] = 0

    def emit(self, eng, fn, inc=None, dma=False):
        amt = 16 if dma else 1
        if inc is not None:
            self.ctr[inc] += amt
        sem = self.sem[inc] if inc is not None else None

        def run(e, fn=fn, sem=sem, amt=amt):
            inst = fn(e)
            if sem is not None:
                inst.then_inc(sem, amt)
        self.ev[eng].append(run)
        return self.ctr[inc] if inc is not None else None

    AUDIT = False

    def wait(self, eng, name, val):
        if val <= 0:
            return
        if Prog.AUDIT and name not in ("te", "ve", "ac") and val != self.ctr[name]:
            print(f"NONCUM wait: eng={eng} sem={name} val={val} cur={self.ctr[name]}")
        sem = self.sem[name]
        self.ev[eng].append(lambda e, s=sem, v=val: e.wait_ge(s, v))

    def replay(self):
        with self.nc.Block() as blk:
            blk.sync(lambda e: [f(e) for f in self.ev["sync"]])
            blk.vector(lambda e: [f(e) for f in self.ev["vector"]])
            blk.scalar(lambda e: [f(e) for f in self.ev["scalar"]])
            blk.tensor(lambda e: [f(e) for f in self.ev["tensor"]])
            blk.gpsimd(lambda e: [f(e) for f in self.ev["gpsimd"]])



def build_l1():
    nc = bass.Bass("TRN2", target_bir_lowering=False, debug=False,
                   detect_race_conditions=False)
    xT = nc.dram_tensor("xT", [128, 16, TC + 3], F32, kind="ExternalInput").ap()
    w_inx = nc.dram_tensor("w_inx", [16, 128, 16, 256], F32, kind="ExternalInput").ap()
    convw = nc.dram_tensor("convw", [128, 32, KCV], F32, kind="ExternalInput").ap()
    w_xp = nc.dram_tensor("w_xp", [128, 32, 160], F32, kind="ExternalInput").ap()
    w_dt = nc.dram_tensor("w_dt", [128, DIN], F32, kind="ExternalInput").ap()
    dt_b = nc.dram_tensor("dt_b", [128, 32], F32, kind="ExternalInput").ap()

    delta_o = nc.dram_tensor("delta_o", [128, 32, TC], F32, kind="ExternalOutput").ap()
    u_o = nc.dram_tensor("u_o", [128, 32, TC], BF16, kind="ExternalOutput").ap()
    xc_o = nc.dram_tensor("xc_o", [128, 32, TC], BF16, kind="ExternalOutput").ap()
    xn_o = nc.dram_tensor("xn_o", [128, 16, TC + 3], BF16, kind="ExternalOutput").ap()
    b_o = nc.dram_tensor("b_o", [16, TC], F32, kind="ExternalOutput").ap()
    c_o = nc.dram_tensor("c_o", [16, TC], F32, kind="ExternalOutput").ap()
    srow = nc.dram_tensor("srow", [2, TB], F32, kind="Internal").ap()

    stack = contextlib.ExitStack()
    with stack:
        sb = lambda n, s, d: stack.enter_context(nc.sbuf_tensor(n, s, d))
        psf = lambda n, s: stack.enter_context(nc.psum_tensor(n, s, F32))
        wq4 = sb("wq4", [128, 16, 1024], BF16)
        fscr = sb("fscr", [128, 5120], F32)
        xpart = sb("xpart", [128, 16, TB + 16], F32)
        bfa = sb("bfa", [128, 16, TB + 16], BF16)
        bfb = sb("bfb", [128, 16, TB + 16], BF16)
        xnb = sb("xnb", [128, 16, TB + 16], BF16)
        xcb = sb("xcb", [128, 8, TB], BF16)
        xcv2 = [sb("xcv0", [128, TB], F32), sb("xcv1", [128, TB], F32)]
        xdbl = sb("xdbl", [128, 2, TC], F32)
        dtr = sb("dtr", [128, TC], BF16)
        wxp = sb("wxp", [128, 32, 160], BF16)
        wdt = sb("wdt", [128, DIN], BF16)
        cw = sb("cw", [128, 32, KCV], F32)
        dtb = sb("dtb", [128, 32], F32)
        stat = sb("stat", [1, 8, TB], F32)
        mrep = sb("mrep", [128, TB], F32)
        rrep = sb("rrep", [128, TB], F32)
        dsc0 = sb("dsc0", [128, TB], F32)
        dsc1 = sb("dsc1", [128, TB], F32)
        dblk = sb("dblk", [128, TB], F32)
        ublk = sb("ublk", [128, TB], BF16)
        onesb = sb("onesb", [128, 1], BF16)
        pA = [psf("pA0", [128, TB]), psf("pA1", [128, TB])]
        pHh = [psf("pH0", [128, 16]), psf("pH1", [128, 16])]
        pst0 = psf("pst0", [1, TB])
        pst1 = psf("pst1", [1, TB])
        px = psf("px", [128, TB])

        p = Prog(nc, stack, extra=("ws", "sw", "xt", "gst", "gxn", "xa", "gxc", "xc0", "xc1", "gout"))

        p.emit("vector", lambda e: e.memset(onesb[:], 1.0), inc="ve")
        p.emit("sync", lambda e: e.dma_start(
            out=fscr[:, 0:5120].rearrange("p (m e) -> p m e", m=32), in_=w_xp), inc="ws", dma=True)
        mw0 = p.ctr["ws"]
        p.wait("vector", "ws", mw0)
        p.emit("vector", lambda e: e.tensor_copy(
            out=wxp[:], in_=fscr[:, 0:5120].rearrange("p (m e) -> p m e", m=32)), inc="ve")
        mwxp = p.ctr["ve"]
        p.wait("sync", "ve", mwxp)
        p.emit("sync", lambda e: e.dma_start(out=fscr[:, 0:4096], in_=w_dt), inc="ws", dma=True)
        mw1 = p.ctr["ws"]
        p.wait("vector", "ws", mw1)
        p.emit("vector", lambda e: e.tensor_copy(out=wdt[:], in_=fscr[:, 0:4096]), inc="ve")
        mwdt = p.ctr["ve"]
        p.emit("sync", lambda e: e.dma_start(out=cw[:], in_=convw), inc="sw", dma=True)
        p.emit("sync", lambda e: e.dma_start(out=dtb[:], in_=dt_b), inc="sw", dma=True)
        msw = p.ctr["sw"]
        p.wait("vector", "sw", msw)
        p.wait("scalar", "sw", msw)

        groups = [(0, TB), (TB, TB), (2 * TB, 3)]
        m_norm_done = 0
        m_stats_used = 0
        m_xn_hbm = 0
        for (c0, w) in groups:
            p.wait("sync", "ve", m_norm_done)
            p.emit("sync", lambda e, c0=c0, w=w: e.dma_start(
                out=xpart[:, 0:8, 0:w], in_=xT[:, 0:8, c0:c0 + w]), inc="xt", dma=True)
            p.emit("sync", lambda e, c0=c0, w=w: e.dma_start(
                out=xpart[:, 8:16, 0:w], in_=xT[:, 8:16, c0:c0 + w]), inc="xt", dma=True)
            mldx = p.ctr["xt"]
            p.wait("vector", "xt", mldx)
            p.emit("vector", lambda e, w=w: e.tensor_copy(out=bfa[:, :, 0:w], in_=xpart[:, :, 0:w]), inc="ve")
            mxbf = p.ctr["ve"]
            p.wait("scalar", "ve", mxbf)
            p.emit("scalar", lambda e, w=w: e.activation(out=bfb[:, :, 0:w], in_=bfa[:, :, 0:w], func=ACT.Square), inc="ac")
            msq = p.ctr["ac"]
            p.wait("tensor", "ve", max(mxbf, 1))
            p.wait("tensor", "ac", msq)
            p.wait("tensor", "ve", m_stats_used)
            for pdst, src in ((pst0, bfa), (pst1, bfb)):
                for k in range(16):
                    p.emit("tensor", lambda e, pdst=pdst, src=src, k=k, w=w: e.matmul(
                        out=pdst[0:1, 0:w], lhsT=onesb[:], rhs=src[:, k, 0:w],
                        start=(k == 0), stop=(k == 15)),
                        inc="te" if (k == 15) else None)
            mst = p.ctr["te"]
            p.wait("vector", "te", mst)
            p.emit("vector", lambda e, w=w: e.tensor_scalar_mul(out=stat[0:1, 0, 0:w], in0=pst0[0:1, 0:w], scalar1=1.0 / D))
            p.emit("vector", lambda e, w=w: e.tensor_scalar_mul(out=stat[0:1, 1, 0:w], in0=pst1[0:1, 0:w], scalar1=1.0 / D))
            p.emit("vector", lambda e, w=w: e.tensor_tensor(out=stat[0:1, 2, 0:w], in0=stat[0:1, 0, 0:w], in1=stat[0:1, 0, 0:w], op=ALU.mult))
            p.emit("vector", lambda e, w=w: e.tensor_tensor(out=stat[0:1, 3, 0:w], in0=stat[0:1, 1, 0:w], in1=stat[0:1, 2, 0:w], op=ALU.subtract))
            p.emit("vector", lambda e, w=w: e.tensor_scalar_add(out=stat[0:1, 3, 0:w], in0=stat[0:1, 3, 0:w], scalar1=EPS), inc="ve")
            mvar = p.ctr["ve"]
            m_stats_used = mvar
            p.wait("scalar", "ve", mvar)
            p.emit("scalar", lambda e, w=w: e.activation(out=stat[0:1, 4, 0:w], in_=stat[0:1, 3, 0:w], func=ACT.Sqrt), inc="ac")
            msqr = p.ctr["ac"]
            p.wait("vector", "ac", msqr)
            p.emit("vector", lambda e, w=w: e.reciprocal(out=stat[0:1, 5, 0:w], in_=stat[0:1, 4, 0:w]), inc="ve")
            mrs = p.ctr["ve"]
            p.wait("gpsimd", "ve", mrs)
            p.emit("gpsimd", lambda e, w=w: e.dma_start(out=srow[0:1, 0:w], in_=stat[0:1, 0, 0:w]), inc="gst", dma=True)
            p.emit("gpsimd", lambda e, w=w: e.dma_start(out=srow[1:2, 0:w], in_=stat[0:1, 5, 0:w]), inc="gst", dma=True)
            mb1 = p.ctr["gst"]
            p.wait("gpsimd", "gst", mb1)
            p.emit("gpsimd", lambda e, w=w: e.dma_start(out=mrep[:, 0:w], in_=srow[0:1, 0:w].partition_broadcast(128)), inc="gst", dma=True)
            p.emit("gpsimd", lambda e, w=w: e.dma_start(out=rrep[:, 0:w], in_=srow[1:2, 0:w].partition_broadcast(128)), inc="gst", dma=True)
            mb2 = p.ctr["gst"]
            p.wait("vector", "gst", mb2)
            for k in range(16):
                p.emit("vector", lambda e, k=k, w=w: e.tensor_tensor(
                    out=xpart[:, k, 0:w], in0=xpart[:, k, 0:w], in1=mrep[:, 0:w], op=ALU.subtract))
                p.emit("vector", lambda e, k=k, w=w: e.tensor_tensor(
                    out=xnb[:, k, 0:w], in0=xpart[:, k, 0:w], in1=rrep[:, 0:w], op=ALU.mult))
            p.emit("vector", lambda e: e.engine_nop(), inc="ve")
            m_norm_done = p.ctr["ve"]
            p.wait("gpsimd", "ve", m_norm_done)
            p.emit("gpsimd", lambda e, c0=c0, w=w: e.dma_start(
                out=xn_o[:, 0:8, c0:c0 + w], in_=xnb[:, 0:8, 0:w]), inc="gxn", dma=True)
            p.emit("gpsimd", lambda e, c0=c0, w=w: e.dma_start(
                out=xn_o[:, 8:16, c0:c0 + w], in_=xnb[:, 8:16, 0:w]), inc="gxn", dma=True)
            m_xn_hbm = p.ctr["gxn"]

        m_wcast = mwdt
        m_xn_used = 0
        m_xc_hbm = 0
        m_xdbl_ve = 0
        m_conv_done = m_norm_done
        evac_marks = [0] * 8
        m_xproj_te = 0
        for q in range(4):
            for g in range(4):
                gj = 4 * q + g
                p.wait("sync", "ve", m_wcast)
                p.emit("sync", lambda e, gj=gj: e.dma_start(
                    out=fscr[:, 0:4096].rearrange("p (k c) -> p k c", k=16), in_=w_inx[gj]), inc="ws", dma=True)
                mw = p.ctr["ws"]
                p.wait("vector", "ws", mw)
                if g == 0:
                    p.wait("vector", "te", m_xn_used)
                p.emit("vector", lambda e, g=g: e.tensor_copy(
                    out=wq4[:, :, 256 * g:256 * (g + 1)], in_=fscr[:, 0:4096].rearrange("p (k c) -> p k c", k=16)), inc="ve")
                m_wcast = p.ctr["ve"]
            for b in range(NB):
                p.wait("sync", "gxn", m_xn_hbm)
                p.wait("sync", "te", m_xn_used)
                p.emit("sync", lambda e, b=b: e.dma_start(
                    out=xnb[:, 0:8, 0:TB + 3], in_=xn_o[:, 0:8, TB * b:TB * b + TB + 3]), inc="xa", dma=True)
                p.emit("sync", lambda e, b=b: e.dma_start(
                    out=xnb[:, 8:16, 0:TB + 3], in_=xn_o[:, 8:16, TB * b:TB * b + TB + 3]), inc="xa", dma=True)
                mxn = p.ctr["xa"]
                p.wait("tensor", "ve", m_wcast)
                p.wait("tensor", "xa", mxn)
                p.wait("tensor", "ve", m_conv_done)
                for m in range(8):
                    sl = m % 2
                    if m >= 2:
                        p.wait("tensor", "ac", evac_marks[m - 2])
                    for k in range(16):
                        p.emit("tensor", lambda e, m=m, k=k, sl=sl: e.matmul(
                            out=pA[sl][:, 0:TB], lhsT=wq4[:, k, 128 * m:128 * (m + 1)],
                            rhs=xnb[:, k, 3:TB + 3], start=(k == 0), stop=(k == 15)))
                    for k in range(16):
                        p.emit("tensor", lambda e, m=m, k=k, sl=sl: e.matmul(
                            out=pHh[sl][:, 0:3], lhsT=wq4[:, k, 128 * m:128 * (m + 1)],
                            rhs=xnb[:, k, 0:3], start=(k == 0), stop=(k == 15)),
                            inc="te" if k == 15 else None)
                    mte = p.ctr["te"]
                    p.wait("scalar", "te", mte)
                    p.emit("scalar", lambda e, m=m, sl=sl: e.activation(
                        out=xpart[:, m, 0:3], in_=pHh[sl][:, 0:3], func=ACT.Identity))
                    p.emit("scalar", lambda e, m=m, sl=sl: e.activation(
                        out=xpart[:, m, 3:TB + 3], in_=pA[sl][:, 0:TB], func=ACT.Identity), inc="ac")
                    evac_marks[m] = p.ctr["ac"]
                m_xn_used = p.ctr["te"]
                m_evac = p.ctr["ac"]
                p.wait("vector", "ac", m_evac)
                p.wait("scalar", "te", m_xproj_te)
                p.wait("scalar", "gxc", m_xc_hbm)
                sil_marks = [0, 0]
                for m in range(8):
                    mc = 8 * q + m
                    xv = xcv2[m % 2]
                    p.wait("vector", "ac", sil_marks[m % 2])
                    p.emit("vector", lambda e, mc=mc, m=m, xv=xv: e.tensor_scalar_mul(
                        out=xv[:, :], in0=xpart[:, m, 0:TB], scalar1=cw[:, mc, 0:1]))
                    for k in range(1, KCV):
                        p.emit("vector", lambda e, m=m, mc=mc, k=k, xv=xv: e.scalar_tensor_tensor(
                            out=xv[:, :], in0=xpart[:, m, k:TB + k], scalar=cw[:, mc, k:k + 1],
                            in1=xv[:, :], op0=ALU.mult, op1=ALU.add), inc="ve" if k == KCV - 1 else None)
                    mcv = p.ctr["ve"]
                    p.wait("scalar", "ve", mcv)
                    p.emit("scalar", lambda e, m=m, xv=xv: e.activation(
                        out=xcb[:, m, :], in_=xv[:, :], func=ACT.Silu), inc="ac")
                    sil_marks[m % 2] = p.ctr["ac"]
                p.emit("vector", lambda e: e.engine_nop(), inc="ve")
                m_conv_done = p.ctr["ve"]
                msil = p.ctr["ac"]
                p.wait("gpsimd", "ac", msil)
                p.emit("gpsimd", lambda e, q=q, b=b: e.dma_start(
                    out=xc_o[:, 8 * q:8 * (q + 1), TB * b:TB * (b + 1)], in_=xcb[:, :, :]),
                    inc="gxc", dma=True)
                m_xc_hbm = p.ctr["gxc"]
                p.wait("tensor", "ac", msil)
                for mo, mwd in ((0, 128), (128, 32)):
                    p.wait("tensor", "ve", m_xdbl_ve)
                    for k in range(8):
                        p.emit("tensor", lambda e, mo=mo, mwd=mwd, k=k, q=q: e.matmul(
                            out=px[0:mwd, 0:TB], lhsT=wxp[:, 8 * q + k, mo:mo + mwd],
                            rhs=xcb[:, k, :], start=(k == 0), stop=(k == 7)),
                            inc="te" if k == 7 else None)
                    mxp = p.ctr["te"]
                    p.wait("vector", "te", mxp)
                    ch = 0 if mo == 0 else 1
                    if q == 0:
                        p.emit("vector", lambda e, ch=ch, mwd=mwd, b=b: e.tensor_copy(
                            out=xdbl[0:mwd, ch, TB * b:TB * (b + 1)], in_=px[0:mwd, 0:TB]), inc="ve")
                    else:
                        p.emit("vector", lambda e, ch=ch, mwd=mwd, b=b: e.tensor_tensor(
                            out=xdbl[0:mwd, ch, TB * b:TB * (b + 1)], in0=xdbl[0:mwd, ch, TB * b:TB * (b + 1)],
                            in1=px[0:mwd, 0:TB], op=ALU.add), inc="ve")
                    m_xdbl_ve = p.ctr["ve"]
                m_xproj_te = p.ctr["te"]

        p.wait("vector", "ve", m_xdbl_ve)
        p.emit("vector", lambda e: e.tensor_copy(out=dtr[:], in_=xdbl[:, 0, :]), inc="ve")
        mdtr = p.ctr["ve"]
        p.wait("gpsimd", "ve", m_xdbl_ve)
        p.emit("gpsimd", lambda e: e.dma_start(out=b_o, in_=xdbl[0:16, 1, :]), inc="gout", dma=True)
        p.emit("gpsimd", lambda e: e.dma_start(out=c_o, in_=xdbl[16:32, 1, :]), inc="gout", dma=True)

        m_px_free = 0
        m_u_prev = 0
        m_store_prev = 0
        for m in range(32):
            for b in range(NB):
                par = (m * NB + b) % 2
                p.wait("sync", "gxc", m_xc_hbm)
                p.wait("sync", "ve", m_u_prev)
                xcsem = "xc0" if par == 0 else "xc1"
                p.emit("sync", lambda e, m=m, b=b, par=par: e.dma_start(
                    out=bfa[:, par, 0:TB], in_=xc_o[:, m, TB * b:TB * (b + 1)]), inc=xcsem, dma=True)
                mxc = p.ctr[xcsem]
                p.wait("tensor", "ve", mdtr)
                p.wait("tensor", "ac", m_px_free)
                p.emit("tensor", lambda e, m=m, b=b: e.matmul(
                    out=px[:, 0:TB], lhsT=wdt[:, 128 * m:128 * (m + 1)],
                    rhs=dtr[:, TB * b:TB * (b + 1)], start=True, stop=True), inc="te")
                mdt = p.ctr["te"]
                p.wait("scalar", "te", mdt)
                p.wait("scalar", "ve", m_u_prev)
                p.emit("scalar", lambda e, m=m: e.activation(
                    out=dsc0[:, :], in_=px[:, 0:TB], func=ACT.Abs, bias=dtb[:, m:m + 1]))
                p.emit("scalar", lambda e, m=m: e.activation(
                    out=dsc1[:, :], in_=px[:, 0:TB], func=ACT.Relu, bias=dtb[:, m:m + 1]), inc="ac")
                m_px_free = p.ctr["ac"]
                p.emit("scalar", lambda e: e.activation(
                    out=dsc0[:, :], in_=dsc0[:, :], func=ACT.Exp, scale=-1.0))
                p.emit("scalar", lambda e: e.activation(
                    out=dsc0[:, :], in_=dsc0[:, :], func=ACT.Ln, bias=1.0), inc="ac")
                mln = p.ctr["ac"]
                p.wait("vector", "ac", mln)
                p.wait("vector", "gout", m_store_prev)
                p.emit("vector", lambda e: e.tensor_tensor(
                    out=dblk[:, :], in0=dsc0[:, :], in1=dsc1[:, :], op=ALU.add), inc="ve")
                p.wait("vector", xcsem, mxc)
                p.emit("vector", lambda e, par=par: e.tensor_tensor(
                    out=ublk[:, :], in0=dblk[:, :], in1=bfa[:, par, 0:TB], op=ALU.mult), inc="ve")
                m_u_prev = p.ctr["ve"]
                p.wait("gpsimd", "ve", m_u_prev)
                p.emit("gpsimd", lambda e, m=m, b=b: e.dma_start(
                    out=delta_o[:, m, TB * b:TB * (b + 1)], in_=dblk[:, :]), inc="gout", dma=True)
                p.emit("gpsimd", lambda e, m=m, b=b: e.dma_start(
                    out=u_o[:, m, TB * b:TB * (b + 1)], in_=ublk[:, :]), inc="gout", dma=True)
                m_store_prev = p.ctr["gout"]

        for nmm in ("gout", "gxn", "gxc", "gst"):
            p.wait("gpsimd", nmm, p.ctr[nmm])
            p.wait("sync", nmm, p.ctr[nmm])
        p.replay()
    return nc



def build_l2():
    W = 4096
    nc = bass.Bass("TRN2", target_bir_lowering=False, debug=False,
                   detect_race_conditions=False)
    delta_i = nc.dram_tensor("delta_i", [128, 4, SEQ], F32, kind="ExternalInput").ap()
    u_i = nc.dram_tensor("u_i", [128, 4, SEQ], BF16, kind="ExternalInput").ap()
    alog_i = nc.dram_tensor("alog_i", [128, 4, NST], F32, kind="ExternalInput").ap()
    b_i = nc.dram_tensor("b_i", [16, SEQ], F32, kind="ExternalInput").ap()
    c_i = nc.dram_tensor("c_i", [16, SEQ], F32, kind="ExternalInput").ap()
    y_o = nc.dram_tensor("y_o", [128, 4, SEQ], F32, kind="ExternalOutput").ap()
    bimg = nc.dram_tensor("bimg", [16, SEQ], BF16, kind="Internal").ap()
    cimg = nc.dram_tensor("cimg", [16, SEQ], BF16, kind="Internal").ap()

    stack = contextlib.ExitStack()
    with stack:
        sb = lambda n, s, d: stack.enter_context(nc.sbuf_tensor(n, s, d))
        dl = sb("dl", [128, W], F32)
        uu = sb("uu", [128, W], BF16)
        dA = sb("dA", [128, W], F32)
        uB = sb("uB", [128, W], BF16)
        h0f = sb("h0f", [128, W], F32)
        h1b = sb("h1b", [128, W], BF16)
        hc0 = sb("hc0", [128, W], BF16)
        hc1 = sb("hc1", [128, W], BF16)
        s2b = sb("s2b", [128, W], BF16)
        s4b = sb("s4b", [128, W], BF16)
        s8b = sb("s8b", [128, W], BF16)
        yb = sb("yb", [128, W], F32)
        brep = [sb("brep0", [128, W], BF16), sb("brep1", [128, W], BF16)]
        crep = [sb("crep0", [128, W], BF16), sb("crep1", [128, W], BF16)]
        bcs = sb("bcs", [16, SEQ // 2], F32)
        bcb = sb("bcb", [16, SEQ // 2], BF16)
        alog = sb("alog", [128, 4, NST], F32)
        aval = sb("aval", [128, 4, NST], F32)
        state = sb("state", [128, 4, NST], F32)

        p = Prog(nc, stack, extra=("sw", "gi", "d1", "u1", "r0", "r1", "gy"))
        p.emit("sync", lambda e: e.dma_start(out=alog[:], in_=alog_i), inc="sw", dma=True)
        ma = p.ctr["sw"]
        p.wait("scalar", "sw", ma)
        p.emit("scalar", lambda e: e.activation(out=aval[:], in_=alog[:], func=ACT.Exp), inc="ac")
        mae = p.ctr["ac"]
        p.wait("vector", "ac", mae)
        p.emit("vector", lambda e: e.tensor_scalar_mul(out=aval[:], in0=aval[:], scalar1=-1.0), inc="ve")
        mave = p.ctr["ve"]
        HS = SEQ // 2
        for src, dst in ((b_i, bimg), (c_i, cimg)):
            for hh in range(2):
                p.wait("sync", "gi", p.ctr["gi"])
                p.emit("sync", lambda e, src=src, hh=hh: e.dma_start(
                    out=bcs[:], in_=src[:, HS * hh:HS * (hh + 1)]), inc="sw", dma=True)
                p.wait("vector", "sw", p.ctr["sw"])
                p.emit("vector", lambda e: e.tensor_copy(out=bcb[:], in_=bcs[:]), inc="ve")
                p.wait("gpsimd", "ve", p.ctr["ve"])
                p.emit("gpsimd", lambda e, dst=dst, hh=hh: e.dma_start(
                    out=dst[:, HS * hh:HS * (hh + 1)], in_=bcb[:]), inc="gi", dma=True)
        m_img = p.ctr["gi"]

        m_dl_used = 0
        m_uu_used = 0
        m_dA_used = 0
        m_rep_used = [0, 0]
        m_y_done = 0
        it = 0
        for c in range(4):
            for w in range(2):
                p.wait("sync", "ac", m_dl_used)
                p.emit("sync", lambda e, c=c, w=w: e.dma_start(
                    out=dl[:], in_=delta_i[:, c, W * w:W * (w + 1)]), inc="d1", dma=True)
                mdl_ = p.ctr["d1"]
                p.wait("sync", "ve", m_uu_used)
                p.emit("sync", lambda e, c=c, w=w: e.dma_start(
                    out=uu[:], in_=u_i[:, c, W * w:W * (w + 1)]), inc="u1", dma=True)
                muu_ = p.ctr["u1"]
                for n in range(NST):
                    sl = it % 2
                    rsem = "r0" if sl == 0 else "r1"
                    p.wait("gpsimd", "gi", m_img)
                    p.wait("gpsimd", "ve", m_rep_used[sl])
                    p.emit("gpsimd", lambda e, n=n, w=w, sl=sl: e.dma_start(
                        out=brep[sl][:], in_=bimg[n:n + 1, W * w:W * (w + 1)].partition_broadcast(128)), inc=rsem, dma=True)
                    p.emit("gpsimd", lambda e, n=n, w=w, sl=sl: e.dma_start(
                        out=crep[sl][:], in_=cimg[n:n + 1, W * w:W * (w + 1)].partition_broadcast(128)), inc=rsem, dma=True)
                    mrep_ = p.ctr[rsem]
                    p.wait("scalar", "d1", mdl_)
                    p.wait("scalar", "ve", max(m_dA_used, mave))
                    p.emit("scalar", lambda e, c=c, n=n: e.activation(
                        out=dA[:], in_=dl[:], func=ACT.Exp, scale=aval[:, c, n:n + 1]), inc="ac")
                    mexp = p.ctr["ac"]
                    if n == NST - 1:
                        m_dl_used = mexp
                    p.wait("vector", "u1", muu_)
                    p.wait("vector", rsem, mrep_)
                    p.emit("vector", lambda e, sl=sl: e.tensor_tensor(
                        out=uB[:], in0=uu[:], in1=brep[sl][:], op=ALU.mult), inc="ve")
                    if n == NST - 1:
                        m_uu_used = p.ctr["ve"]
                    p.wait("vector", "ac", mexp)
                    if w == 0:
                        p.emit("vector", lambda e: e.tensor_tensor_scan(
                            out=h0f[:], data0=dA[:], data1=uB[:], initial=0.0,
                            op0=ALU.mult, op1=ALU.add), inc="ve")
                        m_dA_used = p.ctr["ve"]
                        p.emit("vector", lambda e, c=c, n=n: e.tensor_copy(
                            out=state[:, c, n:n + 1], in_=h0f[:, W - 1:W]))
                        hbuf = h0f
                    else:
                        p.emit("vector", lambda e, c=c, n=n: e.tensor_tensor_scan(
                            out=h1b[:], data0=dA[:], data1=uB[:], initial=state[:, c, n:n + 1],
                            op0=ALU.mult, op1=ALU.add), inc="ve")
                        m_dA_used = p.ctr["ve"]
                        hbuf = h1b
                    hcp = (hc0, hc1)[n % 2]
                    p.emit("vector", lambda e, hbuf=hbuf, hcp=hcp, sl=sl: e.tensor_tensor(
                        out=hcp[:], in0=hbuf[:], in1=crep[sl][:], op=ALU.mult), inc="ve")
                    m_rep_used[sl] = p.ctr["ve"]
                    if n % 2 == 1:
                        if (n // 2) % 2 == 0:
                            p.emit("vector", lambda e: e.tensor_tensor(
                                out=s2b[:], in0=hc0[:], in1=hc1[:], op=ALU.add))
                        else:
                            p.emit("vector", lambda e: e.tensor_tensor(
                                out=hc0[:], in0=hc0[:], in1=hc1[:], op=ALU.add))
                            if (n // 4) % 2 == 0:
                                p.emit("vector", lambda e: e.tensor_tensor(
                                    out=s4b[:], in0=s2b[:], in1=hc0[:], op=ALU.add))
                            else:
                                p.emit("vector", lambda e: e.tensor_tensor(
                                    out=hc0[:], in0=s2b[:], in1=hc0[:], op=ALU.add))
                                if n == 7:
                                    p.emit("vector", lambda e: e.tensor_tensor(
                                        out=s8b[:], in0=s4b[:], in1=hc0[:], op=ALU.add))
                                else:
                                    p.wait("vector", "gy", m_y_done)
                                    p.emit("vector", lambda e: e.tensor_tensor(
                                        out=hc0[:], in0=s4b[:], in1=hc0[:], op=ALU.add))
                                    p.emit("vector", lambda e: e.tensor_tensor(
                                        out=yb[:], in0=s8b[:], in1=hc0[:], op=ALU.add), inc="ve")
                    it += 1
                my = p.ctr["ve"]
                p.wait("gpsimd", "ve", my)
                p.emit("gpsimd", lambda e, c=c, w=w: e.dma_start(
                    out=y_o[:, c, W * w:W * (w + 1)], in_=yb[:]), inc="gy", dma=True)
                m_y_done = p.ctr["gy"]
        for nmm in ("gy", "gi"):
            p.wait("gpsimd", nmm, p.ctr[nmm])
            p.wait("sync", nmm, p.ctr[nmm])
        p.replay()
    return nc



def build_l3():
    TH = 512
    nc = bass.Bass("TRN2", target_bir_lowering=False, debug=False,
                   detect_race_conditions=False)
    xT = nc.dram_tensor("xT", [2, 128, 16, TH], F32, kind="ExternalInput").ap()
    xn_i = nc.dram_tensor("xn_i", [2, 128, 16, TH], BF16, kind="ExternalInput").ap()
    xc_i = nc.dram_tensor("xc_i", [128, 32, TC], BF16, kind="ExternalInput").ap()
    y_i = nc.dram_tensor("y_i", [128, 32, TC], F32, kind="ExternalInput").ap()
    w_inz = nc.dram_tensor("w_inz", [32, 128, 16, 128], F32, kind="ExternalInput").ap()
    w_out = nc.dram_tensor("w_out", [16, 128, 32, 128], F32, kind="ExternalInput").ap()
    wq_g = nc.dram_tensor("wq_g", [44, 128, 16, 128], I32, kind="ExternalInput").ap()
    wq_u = nc.dram_tensor("wq_u", [44, 128, 16, 128], I32, kind="ExternalInput").ap()
    wq_d = nc.dram_tensor("wq_d", [16, 128, 44, 128], I32, kind="ExternalInput").ap()
    s_g = nc.dram_tensor("s_g", [128, 44], F32, kind="ExternalInput").ap()
    s_u = nc.dram_tensor("s_u", [128, 44], F32, kind="ExternalInput").ap()
    s_d = nc.dram_tensor("s_d", [128, 16], F32, kind="ExternalInput").ap()
    dpar = nc.dram_tensor("dpar", [128, 32], F32, kind="ExternalInput").ap()
    o_T = nc.dram_tensor("o_T", [128, 16, TC], F32, kind="ExternalOutput").ap()
    x1sp = nc.dram_tensor("x1sp", [2, 128, 16, TH], F32, kind="Internal").ap()
    srow = nc.dram_tensor("srow", [2, TH], F32, kind="Internal").ap()

    stack = contextlib.ExitStack()
    with stack:
        sb = lambda n, s, d: stack.enter_context(nc.sbuf_tensor(n, s, d))
        psf = lambda n, s: stack.enter_context(nc.psum_tensor(n, s, F32))
        area = sb("area", [128, 44, TH], BF16)
        xn = sb("xn", [128, 16, TH], BF16)
        xn2 = xn
        x1 = sb("x1", [128, 16, TH], F32)
        fstg = [sb("fstg0", [128, 32, 128], F32), sb("fstg1", [128, 32, 128], F32)]
        istg = [sb("istg0", [128, 16, 128], I32), sb("istg1", [128, 16, 128], I32)]
        wt = [sb("wt0", [128, 44, 128], BF16), sb("wt1", [128, 44, 128], BF16)]
        ytl = sb("ytl", [128, TH], F32)
        gtmp = sb("gtmp", [128, TH], F32)
        gtmpb = sb("gtmpb", [128, TH], F32)
        gt = sb("gt", [128, TH], BF16)
        ut = sb("ut", [128, TH], BF16)
        xct = sb("xct", [128, TH], BF16)
        stat = sb("stat", [1, 4, TH], F32)
        mrep = sb("mrep", [128, TH], F32)
        rrep = sb("rrep", [128, TH], F32)
        cs_rep = sb("cs_rep", [128, TH], F32)
        cs2_rep = sb("cs2_rep", [128, TH], F32)
        sgv = sb("sgv", [128, 44], F32)
        suv = sb("suv", [128, 44], F32)
        sdv = sb("sdv", [128, 16], F32)
        sd128 = sb("sd128", [128, 16], F32)
        dp = sb("dp", [128, 32], F32)
        onesb = sb("onesb", [128, 1], BF16)
        onesf = sb("onesf", [128, 1], F32)
        x1r = sb("x1r", [128, TH], F32)
        otile = sb("otile", [128, TH], F32)
        pA = [psf("pA0", [128, TH]), psf("pA1", [128, TH])]
        pst0 = psf("pst0", [1, TH])
        pst1 = psf("pst1", [1, TH])

        p = Prog(nc, stack, extra=("sw", "xa", "f0", "f1", "i0", "i1", "xr", "yx", "gst", "gx1", "go"))
        p.emit("vector", lambda e: e.memset(onesb[:], 1.0), inc="ve")
        p.emit("vector", lambda e: e.memset(onesf[:], 1.0), inc="ve")
        p.emit("sync", lambda e: e.dma_start(out=sgv[:], in_=s_g), inc="sw", dma=True)
        p.emit("sync", lambda e: e.dma_start(out=suv[:], in_=s_u), inc="sw", dma=True)
        p.emit("sync", lambda e: e.dma_start(out=sdv[:], in_=s_d), inc="sw", dma=True)
        p.emit("sync", lambda e: e.dma_start(out=dp[:], in_=dpar), inc="sw", dma=True)
        msetup = p.ctr["sw"]
        p.wait("vector", "sw", msetup)
        p.emit("vector", lambda e: e.tensor_scalar_mul(out=sd128[:], in0=sdv[:], scalar1=-128.0), inc="ve")
        msd = p.ctr["ve"]
        p.wait("scalar", "sw", msetup)

        m_cast = [0, 0]
        m_wused = [0, 0]
        m_evA = [0, 0]
        m_evV = [0, 0]
        m_xn_used = 0
        m_out_done = 0
        m_stats_used = 0
        m_x1r_used = 0
        m_ytl_used = 0
        m_x1sp = 0

        for th in range(2):
            t0 = TH * th
            p.wait("sync", "te", m_xn_used)
            p.emit("sync", lambda e, th=th: e.dma_start(
                out=xn[:, :, :], in_=xn_i[th]), inc="xa", dma=True)
            mxn = p.ctr["xa"]
            def z_load(m):
                sl = m % 2
                fsem = "f0" if sl == 0 else "f1"
                p.wait("sync", "ve", m_cast[sl])
                p.emit("sync", lambda e, m=m, sl=sl: e.dma_start(
                    out=fstg[sl][:, 0:16, :], in_=w_inz[m]), inc=fsem, dma=True)
                mwl = p.ctr[fsem]
                p.wait("vector", fsem, mwl)
                p.wait("vector", "te", m_wused[sl])
                p.emit("vector", lambda e, sl=sl: e.tensor_copy(out=wt[sl][:, 0:16, :], in_=fstg[sl][:, 0:16, :]), inc="ve")
                m_cast[sl] = p.ctr["ve"]
                castm[m] = p.ctr["ve"]
            castm = {}
            z_load(0)
            for m in range(32):
                sl = m % 2
                if m + 1 < 32:
                    z_load(m + 1)
                p.wait("tensor", "ve", castm[m])
                p.wait("tensor", "xa", mxn)
                p.wait("tensor", "ac", m_evA[sl])
                p.wait("tensor", "ve", m_evV[sl])
                for k in range(16):
                    p.emit("tensor", lambda e, k=k, sl=sl: e.matmul(
                        out=pA[sl][:, :], lhsT=wt[sl][:, k, :], rhs=xn[:, k, :],
                        start=(k == 0), stop=(k == 15)), inc="te" if k == 15 else None)
                m_wused[sl] = p.ctr["te"]
                p.wait("scalar", "te", m_wused[sl])
                p.emit("scalar", lambda e, m=m, sl=sl: e.activation(
                    out=area[:, m, :], in_=pA[sl][:, :], func=ACT.Silu), inc="ac")
                m_evA[sl] = p.ctr["ac"]
            msz = p.ctr["ac"]
            m_xn_used = p.ctr["te"]
            for m in range(32):
                p.wait("sync", "ve", m_ytl_used)
                p.emit("sync", lambda e, m=m, t0=t0: e.dma_start(
                    out=ytl[:, :], in_=y_i[:, m, t0:t0 + TH]), inc="yx", dma=True)
                p.emit("sync", lambda e, m=m, t0=t0: e.dma_start(
                    out=xct[:, :], in_=xc_i[:, m, t0:t0 + TH]), inc="yx", dma=True)
                mxcl = p.ctr["yx"]
                p.wait("vector", "yx", mxcl)
                p.wait("vector", "ac", msz)
                p.emit("vector", lambda e, m=m: e.scalar_tensor_tensor(
                    out=gtmp[:, :], in0=xct[:, :], scalar=dp[:, m:m + 1], in1=ytl[:, :],
                    op0=ALU.mult, op1=ALU.add), inc="ve")
                m_ytl_used = p.ctr["ve"]
                p.emit("vector", lambda e, m=m: e.tensor_tensor(
                    out=area[:, m, :], in0=gtmp[:, :], in1=area[:, m, :], op=ALU.mult), inc="ve")
            my2 = p.ctr["ve"]
            def op_load(m):
                sl = m % 2
                fsem = "f0" if sl == 0 else "f1"
                p.wait("sync", "ve", m_cast[sl])
                p.emit("sync", lambda e, m=m, sl=sl: e.dma_start(
                    out=fstg[sl][:], in_=w_out[m]), inc=fsem, dma=True)
                mwl = p.ctr[fsem]
                p.wait("vector", fsem, mwl)
                p.wait("vector", "te", m_wused[sl])
                p.emit("vector", lambda e, sl=sl: e.tensor_copy(out=wt[sl][:, 0:32, :], in_=fstg[sl][:]), inc="ve")
                m_cast[sl] = p.ctr["ve"]
                castm2[m] = p.ctr["ve"]
            castm2 = {}
            op_load(0)
            for m in range(16):
                sl = m % 2
                if m + 1 < 16:
                    op_load(m + 1)
                p.wait("sync", "ve", m_x1r_used)
                p.emit("sync", lambda e, m=m, th=th: e.dma_start(
                    out=x1r[:, :], in_=xT[th, :, m, :]), inc="xr", dma=True)
                mxl = p.ctr["xr"]
                p.wait("tensor", "ve", max(castm2[m], my2))
                p.wait("tensor", "ac", m_evA[sl])
                p.wait("tensor", "ve", m_evV[sl])
                for k in range(32):
                    p.emit("tensor", lambda e, k=k, sl=sl: e.matmul(
                        out=pA[sl][:, :], lhsT=wt[sl][:, k, :], rhs=area[:, k, :],
                        start=(k == 0), stop=(k == 31)), inc="te" if k == 31 else None)
                m_wused[sl] = p.ctr["te"]
                p.wait("vector", "te", m_wused[sl])
                p.wait("vector", "xr", mxl)
                p.emit("vector", lambda e, m=m, sl=sl: e.tensor_tensor(
                    out=x1[:, m, :], in0=pA[sl][:, :], in1=x1r[:, :], op=ALU.add), inc="ve")
                m_evV[sl] = p.ctr["ve"]
                m_x1r_used = p.ctr["ve"]
            mx1 = p.ctr["ve"]
            p.wait("gpsimd", "ve", mx1)
            p.emit("gpsimd", lambda e, th=th: e.dma_start(
                out=x1sp[th], in_=x1[:, :, :]), inc="gx1", dma=True)
            m_x1sp = p.ctr["gx1"]
            p.wait("tensor", "ve", max(mx1, m_stats_used))
            for k in range(16):
                p.emit("tensor", lambda e, k=k: e.matmul(
                    out=pst0[0:1, 0:TH], lhsT=onesf[:], rhs=x1[:, k, :],
                    start=(k == 0), stop=(k == 15)), inc="te" if k == 15 else None)
            for sp2 in range(2):
                c0 = 256 * sp2
                sl = sp2 % 2
                p.wait("scalar", "ve", max(mx1, m_cast[sl]))
                p.emit("scalar", lambda e, c0=c0, sl=sl: e.activation(
                    out=fstg[sl][:].rearrange("p a b -> p (a b)").rearrange("p (a c) -> p a c", a=16),
                    in_=x1[:, :, c0:c0 + 256], func=ACT.Square), inc="ac")
                msq2 = p.ctr["ac"]
                p.wait("tensor", "ac", msq2)
                for k in range(16):
                    p.emit("tensor", lambda e, k=k, c0=c0, sl=sl: e.matmul(
                        out=pst1[0:1, c0:c0 + 256],
                        lhsT=onesf[:],
                        rhs=fstg[sl][:].rearrange("p a b -> p (a b)").rearrange("p (a c) -> p a c", a=16)[:, k, :],
                        start=(k == 0), stop=(k == 15)), inc="te" if k == 15 else None)
                m_cast[sl] = m_cast[sl]
                m_wused[sl] = max(m_wused[sl], p.ctr["te"])
            msx2 = p.ctr["te"]
            p.wait("vector", "te", msx2)
            p.emit("vector", lambda e: e.tensor_scalar_mul(out=stat[0:1, 0, :], in0=pst0[0:1, 0:TH], scalar1=1.0 / D))
            p.emit("vector", lambda e: e.tensor_scalar_mul(out=stat[0:1, 1, :], in0=pst1[0:1, 0:TH], scalar1=1.0 / D))
            p.emit("vector", lambda e: e.tensor_tensor(out=stat[0:1, 2, :], in0=stat[0:1, 0, :], in1=stat[0:1, 0, :], op=ALU.mult))
            p.emit("vector", lambda e: e.tensor_tensor(out=stat[0:1, 2, :], in0=stat[0:1, 1, :], in1=stat[0:1, 2, :], op=ALU.subtract))
            p.emit("vector", lambda e: e.tensor_scalar_add(out=stat[0:1, 2, :], in0=stat[0:1, 2, :], scalar1=EPS), inc="ve")
            mvar = p.ctr["ve"]
            m_stats_used = mvar
            p.wait("scalar", "ve", mvar)
            p.emit("scalar", lambda e: e.activation(out=stat[0:1, 2, :], in_=stat[0:1, 2, :], func=ACT.Sqrt), inc="ac")
            msqr = p.ctr["ac"]
            p.wait("vector", "ac", msqr)
            p.emit("vector", lambda e: e.reciprocal(out=stat[0:1, 2, :], in_=stat[0:1, 2, :]), inc="ve")
            mrs = p.ctr["ve"]
            p.wait("gpsimd", "ve", mrs)
            p.emit("gpsimd", lambda e: e.dma_start(out=srow[0:1, :], in_=stat[0:1, 0, :]), inc="gst", dma=True)
            p.emit("gpsimd", lambda e: e.dma_start(out=srow[1:2, :], in_=stat[0:1, 2, :]), inc="gst", dma=True)
            mb1 = p.ctr["gst"]
            p.wait("gpsimd", "gst", mb1)
            p.emit("gpsimd", lambda e: e.dma_start(out=mrep[:, :], in_=srow[0:1, :].partition_broadcast(128)), inc="gst", dma=True)
            p.emit("gpsimd", lambda e: e.dma_start(out=rrep[:, :], in_=srow[1:2, :].partition_broadcast(128)), inc="gst", dma=True)
            mb2 = p.ctr["gst"]
            p.wait("vector", "gst", mb2)
            p.wait("vector", "te", max(msx2, m_xn_used))
            for k in range(16):
                p.emit("vector", lambda e, k=k: e.tensor_tensor(
                    out=x1[:, k, :], in0=x1[:, k, :], in1=mrep[:, :], op=ALU.subtract))
                p.emit("vector", lambda e, k=k: e.tensor_tensor(
                    out=xn2[:, k, :], in0=x1[:, k, :], in1=rrep[:, :], op=ALU.mult))
            p.emit("vector", lambda e: e.engine_nop(), inc="ve")
            mxn2 = p.ctr["ve"]
            p.wait("tensor", "ve", max(mxn2, m_stats_used))
            for k in range(16):
                p.emit("tensor", lambda e, k=k: e.matmul(
                    out=pst0[0:1, 0:TH], lhsT=onesb[:], rhs=xn2[:, k, :],
                    start=(k == 0), stop=(k == 15)), inc="te" if k == 15 else None)
            mcs = p.ctr["te"]
            p.wait("vector", "te", mcs)
            p.emit("vector", lambda e: e.tensor_scalar_mul(out=stat[0:1, 3, :], in0=pst0[0:1, 0:TH], scalar1=128.0), inc="ve")
            mcsr = p.ctr["ve"]
            m_stats_used = mcsr
            p.wait("gpsimd", "ve", mcsr)
            p.emit("gpsimd", lambda e: e.dma_start(out=srow[0:1, :], in_=stat[0:1, 3, :]), inc="gst", dma=True)
            mb3 = p.ctr["gst"]
            p.wait("gpsimd", "gst", mb3)
            p.emit("gpsimd", lambda e: e.dma_start(out=cs_rep[:, :], in_=srow[0:1, :].partition_broadcast(128)), inc="gst", dma=True)
            mb4 = p.ctr["gst"]
            m_gt_used = 0
            castf = {}

            def fu_load(it):
                m, gi = it // 2, it % 2
                wq = (wq_g, wq_u)[gi]
                sl = it % 2
                isem = "i0" if sl == 0 else "i1"
                p.wait("sync", "ve", m_cast[sl])
                p.emit("sync", lambda e, m=m, sl=sl, wq=wq: e.dma_start(
                    out=istg[sl][:, 0:16, :], in_=wq[m]), inc=isem, dma=True)
                mwl = p.ctr[isem]
                p.wait("vector", isem, mwl)
                p.wait("vector", "te", m_wused[sl])
                p.emit("vector", lambda e, sl=sl: e.tensor_copy(out=wt[sl][:, 0:16, :], in_=istg[sl][:, 0:16, :]), inc="ve")
                m_cast[sl] = p.ctr["ve"]
                castf[it] = p.ctr["ve"]
            fu_load(0)
            fu_load(1)
            for m in range(44):
                for gi in range(2):
                    it = 2 * m + gi
                    sv = (sgv, suv)[gi]
                    fn = (ACT.Silu, ACT.Identity)[gi]
                    sl = it % 2
                    gtp = (gtmp, gtmpb)[gi]
                    p.wait("tensor", "ve", max(castf[it], mxn2))
                    p.wait("tensor", "ac", m_evA[sl])
                    p.wait("tensor", "ve", m_evV[sl])
                    for k in range(16):
                        p.emit("tensor", lambda e, k=k, sl=sl: e.matmul(
                            out=pA[sl][:, :], lhsT=wt[sl][:, k, :], rhs=xn2[:, k, :],
                            start=(k == 0), stop=(k == 15)), inc="te" if k == 15 else None)
                    m_wused[sl] = p.ctr["te"]
                    if it + 2 < 88:
                        fu_load(it + 2)
                    p.wait("vector", "te", m_wused[sl])
                    p.wait("vector", "gst", mb4)
                    p.wait("vector", "ac", m_gt_used)
                    p.emit("vector", lambda e, sl=sl, gtp=gtp: e.tensor_tensor(
                        out=gtp[:, :], in0=pA[sl][:, :], in1=cs_rep[:, :], op=ALU.subtract), inc="ve")
                    m_evV[sl] = p.ctr["ve"]
                    mev = p.ctr["ve"]
                    p.wait("scalar", "ve", mev)
                    if gi == 0:
                        p.emit("scalar", lambda e, m=m, sv=sv, fn=fn, gtp=gtp: e.activation(
                            out=gt[:, :], in_=gtp[:, :], func=fn, scale=sv[:, m:m + 1]), inc="ac")
                    else:
                        p.emit("scalar", lambda e, m=m, sv=sv, fn=fn, gtp=gtp: e.activation(
                            out=ut[:, :], in_=gtp[:, :], func=fn, scale=sv[:, m:m + 1]), inc="ac")
                mgu = p.ctr["ac"]
                p.wait("vector", "ac", mgu)
                p.emit("vector", lambda e, m=m: e.tensor_tensor(
                    out=area[:, m, :], in0=gt[:, :], in1=ut[:, :], op=ALU.mult), inc="ve")
                m_gt_used = mgu
            mguall = p.ctr["ve"]
            p.wait("tensor", "ve", max(mguall, m_stats_used))
            for k in range(44):
                p.emit("tensor", lambda e, k=k: e.matmul(
                    out=pst0[0:1, 0:TH], lhsT=onesb[:], rhs=area[:, k, :],
                    start=(k == 0), stop=(k == 43)), inc="te" if k == 43 else None)
            mcs2 = p.ctr["te"]
            p.wait("vector", "te", mcs2)
            p.emit("vector", lambda e: e.tensor_copy(out=stat[0:1, 3, :], in_=pst0[0:1, 0:TH]), inc="ve")
            mcs2v = p.ctr["ve"]
            m_stats_used = mcs2v
            p.wait("gpsimd", "ve", mcs2v)
            p.emit("gpsimd", lambda e: e.dma_start(out=srow[1:2, :], in_=stat[0:1, 3, :]), inc="gst", dma=True)
            mb5 = p.ctr["gst"]
            p.wait("gpsimd", "gst", mb5)
            p.emit("gpsimd", lambda e: e.dma_start(out=cs2_rep[:, :], in_=srow[1:2, :].partition_broadcast(128)), inc="gst", dma=True)
            mb6 = p.ctr["gst"]
            castd = {}

            def dn_load(m):
                sl = m % 2
                isem = "i0" if sl == 0 else "i1"
                for gof, gn in ((0, 16), (16, 16), (32, 12)):
                    p.wait("sync", "ve", m_cast[sl])
                    p.emit("sync", lambda e, m=m, sl=sl, gof=gof, gn=gn: e.dma_start(
                        out=istg[sl][:, 0:gn, :], in_=wq_d[m, :, gof:gof + gn, :]), inc=isem, dma=True)
                    mwl = p.ctr[isem]
                    p.wait("vector", isem, mwl)
                    if gof == 0:
                        p.wait("vector", "te", m_wused[sl])
                    p.emit("vector", lambda e, sl=sl, gof=gof, gn=gn: e.tensor_copy(
                        out=wt[sl][:, gof:gof + gn, :], in_=istg[sl][:, 0:gn, :]), inc="ve")
                    m_cast[sl] = p.ctr["ve"]
                castd[m] = p.ctr["ve"]
            dn_load(0)
            for m in range(16):
                sl = m % 2
                if m + 1 < 16:
                    dn_load(m + 1)
                p.wait("sync", "ve", m_x1r_used)
                p.wait("sync", "gx1", m_x1sp)
                p.emit("sync", lambda e, m=m, th=th: e.dma_start(
                    out=x1r[:, :], in_=x1sp[th, :, m, :]), inc="xr", dma=True)
                mx1l = p.ctr["xr"]
                p.wait("tensor", "ve", castd[m])
                p.wait("tensor", "ac", m_evA[sl])
                p.wait("tensor", "ve", m_evV[sl])
                for k in range(44):
                    p.emit("tensor", lambda e, k=k, sl=sl: e.matmul(
                        out=pA[sl][:, :], lhsT=wt[sl][:, k, :], rhs=area[:, k, :],
                        start=(k == 0), stop=(k == 43)), inc="te" if k == 43 else None)
                m_wused[sl] = p.ctr["te"]
                p.wait("vector", "te", m_wused[sl])
                p.wait("vector", "gst", mb6)
                p.wait("vector", "xr", mx1l)
                p.wait("vector", "ve", msd)
                p.wait("vector", "go", m_out_done)
                p.emit("vector", lambda e, m=m: e.scalar_tensor_tensor(
                    out=gtmp[:, :], in0=cs2_rep[:, :], scalar=sd128[:, m:m + 1], in1=x1r[:, :],
                    op0=ALU.mult, op1=ALU.add))
                p.emit("vector", lambda e, m=m, sl=sl: e.scalar_tensor_tensor(
                    out=otile[:, :], in0=pA[sl][:, :], scalar=sdv[:, m:m + 1], in1=gtmp[:, :],
                    op0=ALU.mult, op1=ALU.add), inc="ve")
                m_evV[sl] = p.ctr["ve"]
                m_x1r_used = p.ctr["ve"]
                mot = p.ctr["ve"]
                p.wait("gpsimd", "ve", mot)
                p.emit("gpsimd", lambda e, m=m, t0=t0: e.dma_start(
                    out=o_T[:, m, t0:t0 + TH], in_=otile[:, :]), inc="go", dma=True)
                m_out_done = p.ctr["go"]
        for nmm in ("go", "gx1", "gst"):
            p.wait("gpsimd", nmm, p.ctr[nmm])
            p.wait("sync", nmm, p.ctr[nmm])
        p.replay()
    return nc



def _chunk(a, p=128):
    n, t = a.shape
    c = n // p
    return np.ascontiguousarray(a.reshape(c, p, t).transpose(1, 0, 2))


def _unchunk(a):
    p, c, t = a.shape
    return np.ascontiguousarray(a.transpose(1, 0, 2).reshape(c * p, t))


def _vec_chunk(v, p=128):
    return np.ascontiguousarray(v.reshape(-1, p).T)


def _bf(a):
    a = np.asarray(a)
    if a.dtype == np.uint16:
        return a.view(BF)
    return a.astype(BF) if a.dtype != BF else a


def kernel(L0_attn_hidden_states, L0_attn_in_proj_w, L0_attn_conv1d_w,
           L0_attn_x_proj_w, L0_attn_dt_proj_w, L0_attn_dt_proj_bias,
           L0_attn_A_log, L0_attn_D_param, L0_attn_out_proj_w,
           L0_ffn_wq_gate, L0_ffn_s_gate, L0_ffn_wq_up, L0_ffn_s_up,
           L0_ffn_wq_down, L0_ffn_s_down):
    X = np.asarray(L0_attn_hidden_states)[0]
    XT = np.ascontiguousarray(X.T)
    XTp = np.concatenate([np.zeros((D, 3), np.float32), XT], axis=1)

    w_inx0 = _chunk(np.ascontiguousarray(np.asarray(L0_attn_in_proj_w)[:DIN].T))
    w_inx = np.ascontiguousarray(w_inx0.reshape(128, 16, 16, 256).transpose(2, 0, 1, 3))
    convw = _chunk(np.asarray(L0_attn_conv1d_w))
    w_xp = _chunk(np.ascontiguousarray(np.asarray(L0_attn_x_proj_w).T))
    w_dt = np.ascontiguousarray(np.asarray(L0_attn_dt_proj_w).T)
    dt_b = _vec_chunk(np.asarray(L0_attn_dt_proj_bias))

    nc1 = build_l1()
    in1 = []
    for c in range(NC):
        sl = XTp[:, TC * c:TC * c + TC + 3]
        in1.append({
            "xT": _chunk(np.ascontiguousarray(sl)),
            "w_inx": w_inx, "convw": convw, "w_xp": w_xp,
            "w_dt": w_dt, "dt_b": dt_b,
        })
    r1 = run_bass_kernel_spmd(nc1, in1, core_ids=list(range(NC)))
    delta_all = np.concatenate([r1.results[c]["delta_o"] for c in range(NC)], axis=2)
    u_all = np.concatenate([_bf(r1.results[c]["u_o"]) for c in range(NC)], axis=2)
    b_all = np.concatenate([r1.results[c]["b_o"] for c in range(NC)], axis=1)
    c_all = np.concatenate([r1.results[c]["c_o"] for c in range(NC)], axis=1)

    alog = _chunk(np.asarray(L0_attn_A_log).reshape(DIN, NST))
    nc2 = build_l2()
    in2 = []
    for c in range(NC):
        in2.append({
            "delta_i": np.ascontiguousarray(delta_all[:, 4 * c:4 * (c + 1), :]),
            "u_i": np.ascontiguousarray(u_all[:, 4 * c:4 * (c + 1), :]),
            "alog_i": np.ascontiguousarray(alog[:, 4 * c:4 * (c + 1), :]),
            "b_i": b_all, "c_i": c_all,
        })
    r2 = run_bass_kernel_spmd(nc2, in2, core_ids=list(range(NC)))
    y_all = np.concatenate([r2.results[c]["y_o"] for c in range(NC)], axis=1)

    w_inz0 = _chunk(np.ascontiguousarray(np.asarray(L0_attn_in_proj_w)[DIN:].T))
    w_inz = np.ascontiguousarray(w_inz0.reshape(128, 16, 32, 128).transpose(2, 0, 1, 3))
    w_out0 = _chunk(np.ascontiguousarray(np.asarray(L0_attn_out_proj_w).T))
    w_out = np.ascontiguousarray(w_out0.reshape(128, 32, 16, 128).transpose(2, 0, 1, 3))
    wq_g0 = _chunk(np.ascontiguousarray(np.asarray(L0_ffn_wq_gate).T))
    wq_g = np.ascontiguousarray(wq_g0.reshape(128, 16, 44, 128).transpose(2, 0, 1, 3))
    wq_u0 = _chunk(np.ascontiguousarray(np.asarray(L0_ffn_wq_up).T))
    wq_u = np.ascontiguousarray(wq_u0.reshape(128, 16, 44, 128).transpose(2, 0, 1, 3))
    wq_d0 = _chunk(np.ascontiguousarray(np.asarray(L0_ffn_wq_down).T))
    wq_d = np.ascontiguousarray(wq_d0.reshape(128, 44, 16, 128).transpose(2, 0, 1, 3))
    s_g = _vec_chunk(np.asarray(L0_ffn_s_gate))
    s_u = _vec_chunk(np.asarray(L0_ffn_s_up))
    s_d = _vec_chunk(np.asarray(L0_ffn_s_down))
    dpar = _vec_chunk(np.asarray(L0_attn_D_param))

    nc3 = build_l3()
    in3 = []
    for c in range(NC):
        in3.append({
            "xT": np.ascontiguousarray(_chunk(np.ascontiguousarray(XT[:, TC * c:TC * (c + 1)])).reshape(128, 16, 2, 512).transpose(2, 0, 1, 3)),
            "xn_i": np.ascontiguousarray(_bf(r1.results[c]["xn_o"])[:, :, 3:].reshape(128, 16, 2, 512).transpose(2, 0, 1, 3)),
            "xc_i": _bf(r1.results[c]["xc_o"]),
            "y_i": np.ascontiguousarray(y_all[:, :, TC * c:TC * (c + 1)]),
            "w_inz": w_inz, "w_out": w_out,
            "wq_g": wq_g, "wq_u": wq_u, "wq_d": wq_d,
            "s_g": s_g, "s_u": s_u, "s_d": s_d, "dpar": dpar,
        })
    r3 = run_bass_kernel_spmd(nc3, in3, core_ids=list(range(NC)))
    oT = np.concatenate([_unchunk(r3.results[c]["o_T"]) for c in range(NC)], axis=1)
    return np.ascontiguousarray(oT.T).reshape(1, SEQ, D).astype(np.float32)
